# version 18 (final)
# baseline (speedup 1.0000x reference)
import os
import numpy as np
import ml_dtypes
from contextlib import ExitStack

import concourse.bass as bass
import concourse.mybir as mybir
from concourse.bass_utils import run_bass_kernel_spmd

F32 = mybir.dt.float32
BF16 = mybir.dt.bfloat16
I8 = mybir.dt.int8
Alu = mybir.AluOpType
Act = mybir.ActivationFunctionType
AX = mybir.AxisListType
BFD = ml_dtypes.bfloat16

B, T, D, HFF = 4, 2048, 2048, 8192
NH, KVH, HD = 32, 8, 64
TC = 1024
NSTRIP = 8
NDC = 16
NFC = 64
QQ = 127.0
C32 = float(1.5 * 2.0 ** 23)
C64 = float(1.5 * 2.0 ** 52)

SLOTS = [(0, s) for s in range(8)] + [(1, s - 8) for s in range(8, 24)]
MASKED = sorted(set(range(0, 8)) | set(range(16, 24)))
MASKIDX = {s: i for i, s in enumerate(MASKED)}
BATCHES = [[0, 1, 2], [3, 4, 5], [6, 7], [8, 9, 10], [11, 12, 13], [14, 15, 16],
           [17, 18, 19], [20, 21, 22], [23]]

AR_GROUPS = [list(range(8))]
DEBUG_DUMPS = False
AG_GROUPS = [[0, 1], [2, 3], [4, 5], [6, 7]]


def _pos_list(j):
    g1 = np.arange(512 * j, 512 * (j + 1))
    g2 = np.arange(2048 - 512 * (j + 1), 2048 - 512 * j)
    return np.concatenate([g1, g2])


class Sem:
    def __init__(self, E, nc, name):
        self.h = E(nc.semaphore(name))
        self.n = 0

    def inc(self, instr, k=1):
        instr.then_inc(self.h, k)
        self.n += k
        return self.n


def build_program():
    nc = bass.Bass("TRN2", target_bir_lowering=False)

    d_xg = nc.dram_tensor("xg", [TC, D], F32, kind="ExternalInput").ap()
    d_w1s = nc.dram_tensor("w1s", [8, 128, D], F32, kind="ExternalInput").ap()
    d_w2s = nc.dram_tensor("w2s", [2, 128, HFF], F32, kind="ExternalInput").ap()
    d_was = nc.dram_tensor("was", [5, 128, D], F32, kind="ExternalInput").ap()
    d_b1 = nc.dram_tensor("b1t", [128, NFC], F32, kind="ExternalInput").ap()
    d_b2 = nc.dram_tensor("b2t", [128, NDC], F32, kind="ExternalInput").ap()
    d_bq = nc.dram_tensor("bqt", [128, 16], F32, kind="ExternalInput").ap()
    d_bk = nc.dram_tensor("bkt", [128, 4], F32, kind="ExternalInput").ap()
    d_bv = nc.dram_tensor("bvt", [128, 4], F32, kind="ExternalInput").ap()
    d_bo = nc.dram_tensor("bor", [1, D], F32, kind="ExternalInput").ap()
    d_cos = nc.dram_tensor("cosb", [128, TC], BF16, kind="ExternalInput").ap()
    d_sin = nc.dram_tensor("sinb", [128, TC], BF16, kind="ExternalInput").ap()
    d_idf = nc.dram_tensor("idf", [128, 128], F32, kind="ExternalInput").ap()
    d_idb = nc.dram_tensor("idb", [128, 128], BF16, kind="ExternalInput").ap()
    d_rot = nc.dram_tensor("rotT", [128, 128], BF16, kind="ExternalInput").ap()
    d_msk = nc.dram_tensor("msk", [16, 128, 512], BF16, kind="ExternalInput").ap()

    d_out = nc.dram_tensor("outb", [TC, D], F32, kind="ExternalOutput").ap()
    d_waqs = nc.dram_tensor("waqs", [5, 128, D], I8, kind="Internal").ap()
    d_waq8 = nc.dram_tensor("waq8", [5120, D], I8, kind="Internal").ap()
    d_w1qsA = nc.dram_tensor("w1qsA", [4, 128, D], I8, kind="Internal").ap()
    d_w1qsB = nc.dram_tensor("w1qsB", [4, 128, D], I8, kind="Internal").ap()
    d_w2qs = nc.dram_tensor("w2qs", [2, 128, HFF], I8, kind="Internal").ap()
    d_w1q8A = nc.dram_tensor("w1q8A", [32, 128, D], I8, kind="Internal").ap()
    d_w1q8B = nc.dram_tensor("w1q8B", [32, 128, D], I8, kind="Internal").ap()
    d_w2q8 = nc.dram_tensor("w2q8", [NDC, 128, HFF], I8, kind="Internal").ap()
    d_x1b = nc.dram_tensor("x1b", [NDC, NSTRIP, 128, 128], F32, kind="Internal").ap()
    d_x2f = nc.dram_tensor("x2f", [NDC, NSTRIP, 128, 128], F32, kind="Internal").ap()
    d_x1t = nc.dram_tensor("x1t", [TC, D], F32, kind="Internal").ap()
    d_x2t = nc.dram_tensor("x2t", [TC, D], F32, kind="Internal").ap()
    d_agi = nc.dram_tensor("agi", [TC, TC], BF16, kind="Internal").ap()
    d_ago = nc.dram_tensor("ago", [2 * TC, TC], BF16, kind="Internal").ap()
    d_ari = nc.dram_tensor("ari", [1, 4], F32, kind="Internal").ap()
    d_aro = nc.dram_tensor("aro", [1, 4], F32, kind="Internal").ap()
    d_dbg1 = nc.dram_tensor("dbgx1q", [NDC, 128, TC], BF16, kind="Internal").ap() if DEBUG_DUMPS else None
    d_dbg2 = nc.dram_tensor("dbgx2", [NFC, 128, TC], BF16, kind="Internal").ap() if DEBUG_DUMPS else None
    d_dbg3 = nc.dram_tensor("dbgx2q", [NFC, 128, TC], BF16, kind="Internal").ap() if DEBUG_DUMPS else None

    ctx = ExitStack()
    E = ctx.enter_context
    with ctx:
        arA = E(nc.sbuf_tensor("arA", [128, 65536], BF16))
        arB = E(nc.sbuf_tensor("arB", [128, 16384], F32))
        _A, _B = arA.ap(), arB.ap()
        arAf = _A.bitcast(F32)
        arBb = _B.bitcast(BF16)

        def A(c0, n):
            return _A[:, c0:c0 + n]

        def Af(c0, n):
            return arAf[:, c0:c0 + n]

        def Bf(c0, n):
            return _B[:, c0:c0 + n]

        def Bb(c0, n):
            return arBb[:, c0:c0 + n]

        def COSB():
            return Bb(6144, 1024)

        def SINB():
            return Bb(7168, 1024)

        def QN2P():
            return Bf(13440, 1024)

        def MRQ():
            return Bb(28928, 1024)

        def ATTCH(sl4):
            return Bb(16384 + sl4 * 1536, 1536)

        def REC64():
            return Bf(5696, 512)

        def AODD():
            return Bb(12416, 512)

        def RECB():
            return Bf(6464, 512)

        def REC64B():
            return Bb(29952, 512)

        def BOBC():
            return Bf(0, 2048)

        def BOROW():
            return Bf(2048, 2048)[0:1, :]

        def S1F():
            return Bf(1024, 1024)[0:1, :]

        def S2F():
            return Bf(2048, 1024)[0:1, :]

        def QROW1():
            return Af(24576, 1024)[0:1, :]

        def QROW2():
            return Af(25600, 1024)[0:1, :]

        def S1BCF1():
            return Bb(15360, 1024)

        def TT1():
            return Bb(14336, 1024)

        def S1BCF4():
            return Bb(28672, 1024)

        def BLK4(sl):
            return Bf(12288 + sl * 1024, 1024)

        def S2BC():
            return Bf(8192, 1024)

        identf = E(nc.sbuf_tensor("identf", [128, 128], F32))
        identb = E(nc.sbuf_tensor("identb", [128, 128], BF16))
        rotT = E(nc.sbuf_tensor("rotTt", [128, 128], BF16))
        b1t = E(nc.sbuf_tensor("b1tt", [128, NFC], F32))
        b2h = E(nc.sbuf_tensor("b2ht", [128, NDC], F32))
        bqt = E(nc.sbuf_tensor("bqtt", [128, 16], F32))
        bkt = E(nc.sbuf_tensor("bktt", [128, 4], F32))
        bvt = E(nc.sbuf_tensor("bvtt", [128, 4], F32))
        onesc = E(nc.sbuf_tensor("onesc", [128, 1], F32))
        ones64 = E(nc.sbuf_tensor("ones64", [128, 128], F32))
        cC = E(nc.sbuf_tensor("cCt", [128, 1], F32))
        epsl = E(nc.sbuf_tensor("epsl", [128, 1], F32))
        sw1s = E(nc.sbuf_tensor("sw1s", [128, 1], F32))
        sw2s = E(nc.sbuf_tensor("sw2s", [128, 1], F32))
        sw1c = E(nc.sbuf_tensor("sw1c", [128, 1], F32))
        sw2c = E(nc.sbuf_tensor("sw2c", [128, 1], F32))
        swats = E(nc.sbuf_tensor("swats", [128, 1], F32))
        swatd = E(nc.sbuf_tensor("swatd", [128, 1], F32))
        runmax = E(nc.sbuf_tensor("runmax", [128, TC], BF16))
        runmin = E(nc.sbuf_tensor("runmin", [128, TC], BF16))
        p8 = E(nc.sbuf_tensor("p8", [128, 128], F32))
        p8b = E(nc.sbuf_tensor("p8b", [128, 128], F32))
        lnm = E(nc.sbuf_tensor("lnm", [128, NSTRIP], F32))
        lnr = E(nc.sbuf_tensor("lnr", [128, NSTRIP], F32))
        lna = E(nc.sbuf_tensor("lna", [128, NSTRIP], F32))
        lns = E(nc.sbuf_tensor("lns", [128, NSTRIP], F32))
        lnx = E(nc.sbuf_tensor("lnx", [128, NSTRIP], F32))
        wabs = E(nc.sbuf_tensor("wabs", [1, 4], F32))
        armx = E(nc.sbuf_tensor("armx", [1, 4], F32))
        blk = E(nc.sbuf_tensor("blk", [128, 1024], F32))
        onesb = E(nc.sbuf_tensor("onesb", [128, 128], BF16))

        psS = E(nc.psum_tensor("psS", [128, 1536], F32))
        psS2 = E(nc.psum_tensor("psS2", [128, 1536], F32))
        psAV = E(nc.psum_tensor("psAV", [128, 512], F32))
        psX = E(nc.psum_tensor("psX", [128, 512], F32))
        psTb = psS.ap().bitcast(BF16)[:, 2048:3072]
        psTb2 = psS2.ap().bitcast(BF16)[:, 2048:3072]

        sms = {n: Sem(E, nc, n) for n in
               ("ds", "dsl", "dso", "dg", "dgo", "dga", "tm", "vx", "sx", "cc",
                "vq0", "vq1", "cq0", "cq1", "tq0", "tq1")}
        ds, dsl, dso = sms["ds"], sms["dsl"], sms["dso"]
        dg, dgo = sms["dg"], sms["dgo"]
        tm, vx, sx, cc = sms["tm"], sms["vx"], sms["sx"], sms["cc"]
        dga = sms["dga"]
        vq = (sms["vq0"], sms["vq1"])
        cq = (sms["cq0"], sms["cq1"])
        tq = (sms["tq0"], sms["tq1"])

        block = E(nc.Block())

        def barrier(with_queues=True):
            t_n, v_n, s_n = tm.n + 1, vx.n + 1, sx.n + 1
            if isinstance(with_queues, tuple):
                q = {n: sms[n].n for n in with_queues}
                qsel = [(sms[n], q[n]) for n in with_queues]
                with_queues = 'sel'
            else:
                q = dict(ds=ds.n, dsl=dsl.n, dso=dso.n, dg=dg.n, dgo=dgo.n)

            @block.tensor
            def _(t):
                t.drain()
                tm.inc(t.nop())
                t.wait_ge(vx.h, v_n)
                t.wait_ge(sx.h, s_n)
                if with_queues == 'sel':
                    for s_, v_ in qsel:
                        t.wait_ge(s_.h, v_)
                elif with_queues:
                    for s_, v_ in ((ds, q['ds']), (dsl, q['dsl']), (dso, q['dso']),
                                   (dg, q['dg']), (dgo, q['dgo'])):
                        t.wait_ge(s_.h, v_)

            @block.vector
            def _(v):
                v.drain()
                vx.inc(v.engine_nop())
                v.wait_ge(tm.h, t_n)
                v.wait_ge(sx.h, s_n)
                if with_queues == 'sel':
                    for s_, v_ in qsel:
                        v.wait_ge(s_.h, v_)
                elif with_queues:
                    for s_, v_ in ((ds, q['ds']), (dsl, q['dsl']), (dso, q['dso']),
                                   (dg, q['dg']), (dgo, q['dgo'])):
                        v.wait_ge(s_.h, v_)

            @block.scalar
            def _(s):
                s.drain()
                sx.inc(s.nop())
                s.wait_ge(tm.h, t_n)
                s.wait_ge(vx.h, v_n)
                if with_queues == 'sel':
                    for s_, v_ in qsel:
                        s.wait_ge(s_.h, v_)
                elif with_queues:
                    for s_, v_ in ((ds, q['ds']), (dsl, q['dsl']), (dso, q['dso']),
                                   (dg, q['dg']), (dgo, q['dgo'])):
                        s.wait_ge(s_.h, v_)

            @block.sync
            def _(s):
                s.wait_ge(tm.h, t_n)
                s.wait_ge(vx.h, v_n)
                s.wait_ge(sx.h, s_n)
                if with_queues == 'sel':
                    for s_, v_ in qsel:
                        s.wait_ge(s_.h, v_)
                elif with_queues:
                    for s_, v_ in ((dsl, q['dsl']), (dso, q['dso']),
                                   (dg, q['dg']), (dgo, q['dgo'])):
                        s.wait_ge(s_.h, v_)

            @block.gpsimd
            def _(g):
                g.wait_ge(tm.h, t_n)
                g.wait_ge(vx.h, v_n)
                g.wait_ge(sx.h, s_n)
                if with_queues == 'sel':
                    for s_, v_ in qsel:
                        g.wait_ge(s_.h, v_)
                elif with_queues:
                    for s_, v_ in ((ds, q['ds']), (dsl, q['dsl']), (dso, q['dso'])):
                        g.wait_ge(s_.h, v_)

        ST = {}

        def X1Q(dc):
            return Bb(16384 + dc * 1024, 1024)

        def X2(fc):
            return A(fc * 1024, 1024)

        @block.sync
        def _(s):
            for tc_ in range(NSTRIP):
                ds.inc(s.dma_start(out=Bf(tc_ * 2048, 2048), in_=d_xg[tc_ * 128:(tc_ + 1) * 128, :]), 16)
            for dst, src in ((identf[:, :], d_idf), (identb[:, :], d_idb), (rotT[:, :], d_rot),
                             (b1t[:, :], d_b1), (b2h[:, :], d_b2), (bqt[:, :], d_bq),
                             (bkt[:, :], d_bk), (bvt[:, :], d_bv)):
                ds.inc(s.dma_start(out=dst, in_=src), 16)
        P0_LOADS = ds.n

        @block.vector
        def _(v):
            v.memset(cC[:, :], 2.0 ** -30)
            v.memset(epsl[:, :], 1e-5)
            v.memset(onesc[:, :], 1.0)
            v.memset(ones64[:, :], 1.0)
            v.memset(onesb[:, :], 1.0)
            v.memset(runmax[:, :], -3.0e38)
            v.memset(runmin[:, :], 3.0e38)
            v.memset(p8[:, :], 0.0)
            v.memset(p8b[:, :], 0.0)
            v.drain()
            v.tensor_scalar(cC[:, :], cC[:, :], 1.0, 1.0, Alu.add, Alu.subtract)
            v.drain()
            v.tensor_scalar(cC[:, :], cC[:, :], float((C64 - C32) * (2.0 ** 30)), C32, Alu.mult, Alu.add)
            v.drain()
            v.wait_ge(ds.h, P0_LOADS)
            v.tensor_scalar_mul(b2h[:, :], b2h[:, :], 0.5)
            v.drain()
            vx.inc(v.engine_nop())
        V_CONST = vx.n

        @block.gpsimd
        def _(g):
            g.wait_ge(vx.h, V_CONST)
            for i in range(5):
                dg.inc(g.dma_start(out=Af(i * 2048, 2048), in_=d_was[i]), 16)
        DG_WA = dg.n

        @block.sync
        def _(s):
            s.wait_ge(vx.h, V_CONST)
            for i in range(8):
                dsl.inc(s.dma_start(out=Af(10240 + i * 2048, 2048), in_=d_w1s[i]), 16)
        DSL_W1 = dsl.n

        w2ld, w2v = {}, {}
        for k in range(8):
            @block.scalar
            def _(s, k=k):
                sm_ = cq[k % 2]
                if k == 0:
                    s.wait_ge(vx.h, V_CONST)
                if k >= 2:
                    s.wait_ge(vx.h, w2v[k - 2])
                    s.wait_ge(sm_.h, sm_.n)
                sm_.inc(s.dma_start(out=Af(26624 + (k % 2) * 2048, 2048),
                                    in_=d_w2s[k // 4][:, (k % 4) * 2048:(k % 4 + 1) * 2048]), 16)
            w2ld[k] = (cq[k % 2].h, cq[k % 2].n)

            @block.vector
            def _(v, k=k):
                v.wait_ge(w2ld[k][0], w2ld[k][1])
                if k == 0:
                    v.memset(psS[:, 0:512], -3.0e38)
                    v.memset(psS2[:, 0:512], 3.0e38)
                    v.drain()
                for c in range(4):
                    v.tensor_tensor(psS[:, 0:512], psS[:, 0:512],
                                    Af(26624 + (k % 2) * 2048 + c * 512, 512), Alu.max)
                    v.drain()
                    v.tensor_tensor(psS2[:, 0:512], psS2[:, 0:512],
                                    Af(26624 + (k % 2) * 2048 + c * 512, 512), Alu.min)
                    v.drain()
                vx.inc(v.engine_nop())
            w2v[k] = vx.n

        @block.vector
        def _(v):
            def chain(base, n, col, pmx, pmn):
                v.memset(pmx[:, 0:512], -3.0e38)
                v.memset(pmn[:, 0:512], 3.0e38)
                v.drain()
                for k in range(n):
                    v.tensor_tensor(pmx[:, 0:512], pmx[:, 0:512], Af(base + k * 512, 512), Alu.max)
                    v.drain()
                    v.tensor_tensor(pmn[:, 0:512], pmn[:, 0:512], Af(base + k * 512, 512), Alu.min)
                    v.drain()

            def absred(dst_ap, pmx, pmn):
                v.tensor_copy(out=blk[:, 512:1024], in_=pmn[:, 0:512])
                v.drain()
                v.scalar_tensor_tensor(blk[:, 0:512], blk[:, 512:1024], -1.0, pmx[:, 0:512],
                                       Alu.mult, Alu.max)
                v.drain()
                v.reduce_max(dst_ap, blk[:, 0:512], AX.X)
                v.drain()

            v.wait_ge(dsl.h, DSL_W1)
            chain(10240, 32, 0, psX, psAV)
            absred(p8[:, 0:1], psX, psAV)
            v.wait_ge(dg.h, DG_WA)
            chain(0, 20, 2, psX, psAV)
            absred(p8b[:, 1:2], psX, psAV)
            v.wait_ge(vx.h, w2v[7])
            absred(p8b[:, 0:1], psS, psS2)
            v.drain()
            vx.inc(v.engine_nop())
        V_WABS0 = vx.n

        @block.tensor
        def _(t):
            t.wait_ge(vx.h, V_WABS0)
            t.wait_ge(ds.h, P0_LOADS)
            tm.inc(t.transpose(out=psX[:, 0:128], in_=p8[:, 0:128], identity=identf[:, :]))
            tm.inc(t.transpose(out=psX[:, 128:256], in_=p8b[:, 0:128], identity=identf[:, :]))
        T_WTP = tm.n

        @block.vector
        def _(v):
            v.wait_ge(tm.h, T_WTP)
            v.reduce_max(wabs[0:1, 0:1], psX[0:1, 0:128], AX.X)
            v.reduce_max(wabs[0:1, 1:2], psX[0:1, 128:256], AX.X)
            v.drain()
            v.memset(wabs[0:1, 2:4], 0.0)
            v.drain()
            v.tensor_copy(out=p8[:, 0:1], in_=p8b[:, 1:2])
            v.drain()
            vx.inc(v.engine_nop())
        V_WABS1 = vx.n

        @block.tensor
        def _(t):
            t.wait_ge(vx.h, V_WABS1)
            tm.inc(t.transpose(out=psX[:, 256:384], in_=p8[:, 0:128], identity=identf[:, :]))
        T_WTP2 = tm.n

        @block.vector
        def _(v):
            v.wait_ge(tm.h, T_WTP2)
            v.reduce_max(wabs[0:1, 2:3], psX[0:1, 256:384], AX.X)
            v.drain()
            vx.inc(v.engine_nop())
        V_WABS = vx.n

        @block.gpsimd
        def _(g):
            g.wait_ge(vx.h, V_WABS)
            dg.inc(g.dma_start(out=d_ari, in_=wabs[0:1, :]), 16)
            g.wait_ge(dg.h, dg.n)
            cc.inc(g.collective_compute("AllReduce", Alu.max, replica_groups=AR_GROUPS,
                                        ins=[d_ari], outs=[d_aro]))
            g.wait_ge(cc.h, cc.n)
            dg.inc(g.dma_start(out=armx[0:1, :], in_=d_aro), 16)
        G_ARMX = dg.n

        @block.vector
        def _(v):
            v.wait_ge(dg.h, G_ARMX)
            v.reciprocal(p8[0:1, 0:3], armx[0:1, 0:3])
            v.drain()
            v.tensor_scalar_mul(p8[0:1, 0:3], p8[0:1, 0:3], QQ)
            v.drain()
            v.tensor_scalar_mul(p8[0:1, 3:5], armx[0:1, 0:2], float(1.0 / (QQ * QQ)))
            v.drain()
            v.tensor_scalar_mul(p8[0:1, 5:6], armx[0:1, 2:3], float(1.0 / QQ))
            v.drain()
            vx.inc(v.engine_nop())
        V_WSC = vx.n

        @block.tensor
        def _(t):
            t.wait_ge(vx.h, V_WSC)
            tm.inc(t.matmul(out=psX[:, 0:8], lhsT=ones64[0:1, :], rhs=p8[0:1, 0:8], start=True, stop=True))
        T_WSC = tm.n

        @block.vector
        def _(v):
            v.wait_ge(tm.h, T_WSC)
            v.tensor_copy(out=sw1s[:, :], in_=psX[:, 0:1])
            v.tensor_copy(out=sw2s[:, :], in_=psX[:, 1:2])
            v.tensor_copy(out=swats[:, :], in_=psX[:, 2:3])
            v.tensor_copy(out=sw1c[:, :], in_=psX[:, 3:4])
            v.tensor_copy(out=sw2c[:, :], in_=psX[:, 4:5])
            v.tensor_copy(out=swatd[:, :], in_=psX[:, 5:6])
            v.drain()
            vx.inc(v.engine_nop())
        V_WSC2 = vx.n

        qz_s, qz_v, qz_g, qz_ld = {}, {}, {}, {}
        blkb = blk.ap().bitcast(BF16)

        def QSTG(u):
            return blkb[:, 0:2048] if u % 2 == 0 else A(61440, 2048)
        units = ([('1', i) for i in range(8)] + [('a', i) for i in range(5)]
                 + [('2', i) for i in range(8)])
        for u, (kind, i) in enumerate(units):
            if kind == 'a':
                srcf, sc = Af(i * 2048, 2048), swats
            elif kind == '1':
                srcf, sc = Af(10240 + i * 2048, 2048), sw1s
            else:
                srcf, sc = Af((i % 2) * 2048, 2048), sw2s

            if kind == '2':
                @block.scalar
                def _(s, u=u, i=i, srcf=srcf):
                    sm_ = cq[i % 2]
                    if i < 2:
                        s.wait_ge(dgo.h, qz_g[9])
                    else:
                        s.wait_ge(dgo.h, qz_g[13 + (i - 2)])
                    s.wait_ge(sm_.h, sm_.n)
                    sm_.inc(s.dma_start(out=srcf,
                                        in_=d_w2s[i // 4][:, (i % 4) * 2048:(i % 4 + 1) * 2048]), 16)
                qz_ld[u] = (cq[i % 2].h, cq[i % 2].n)

            @block.scalar
            def _(s, u=u, kind=kind, sc=sc, srcf=srcf):
                if u == 0:
                    s.wait_ge(vx.h, V_WSC2)
                if kind == '2':
                    s.wait_ge(qz_ld[u][0], qz_ld[u][1])
                sx.inc(s.activation(out=srcf, in_=srcf, func=Act.Identity,
                                    bias=cC[:, :], scale=sc[:, :]))
            qz_s[u] = sx.n

            @block.vector
            def _(v, u=u, srcf=srcf):
                v.wait_ge(sx.h, qz_s[u])
                if u >= 2:
                    v.wait_ge(dgo.h, qz_g[u - 2])
                vx.inc(v.tensor_scalar_sub(QSTG(u), srcf, cC[:, :]))
            qz_v[u] = vx.n

            @block.gpsimd
            def _(g, u=u, kind=kind, i=i):
                g.wait_ge(vx.h, qz_v[u])
                g.wait_ge(dgo.h, dgo.n)
                if kind == 'a':
                    dst = d_waqs[i]
                elif kind == '1':
                    dst = d_w1qsA[i] if i < 4 else d_w1qsB[i - 4]
                else:
                    dst = d_w2qs[i // 4][:, (i % 4) * 2048:(i % 4 + 1) * 2048]
                dgo.inc(g.dma_start(out=dst, in_=QSTG(u)), 16)
            qz_g[u] = dgo.n

        @block.gpsimd
        def _(g):
            g.wait_ge(dgo.h, qz_g[3])
            cc.inc(g.collective_compute("AllGather", Alu.bypass, replica_groups=AR_GROUPS,
                                        ins=[d_w1qsA], outs=[d_w1q8A]))
        ST['ag_w1a'] = cc.n

        @block.gpsimd
        def _(g):
            g.wait_ge(dgo.h, qz_g[7])
            cc.inc(g.collective_compute("AllGather", Alu.bypass, replica_groups=AR_GROUPS,
                                        ins=[d_w1qsB], outs=[d_w1q8B]))
        ST['ag_w1'] = cc.n

        @block.gpsimd
        def _(g):
            g.wait_ge(dgo.h, qz_g[12])
            cc.inc(g.collective_compute("AllGather", Alu.bypass, replica_groups=AR_GROUPS,
                                        ins=[d_waqs], outs=[d_waq8.rearrange("(a b) c -> a b c", b=128)]))
        ST['ag_wa'] = cc.n

        @block.gpsimd
        def _(g):
            g.wait_ge(dgo.h, qz_g[20])
            cc.inc(g.collective_compute("AllGather", Alu.bypass, replica_groups=AR_GROUPS,
                                        ins=[d_w2qs], outs=[d_w2q8]))
        ST['ag_w2'] = cc.n

        barrier()

        def emit_blkadd(call):
            src_blk = d_x1b if call == 1 else d_x2f
            res_rows = d_xg if call == 1 else d_x2t
            dst_rows = d_x1t if call == 1 else d_out

            def DST(k):
                return Af(8192 + (k % 16) * 128, 128)

            @block.sync
            def _(s):
                for tc_ in range(NSTRIP):
                    ds.inc(s.dma_start(out=Bf(tc_ * 2048, 2048),
                                       in_=res_rows[tc_ * 128:(tc_ + 1) * 128, :]), 16)
            BA_RES = ds.n

            t0 = tm.n
            v0 = vx.n
            ld_sems = []

            @block.sync
            def _(s):
                for b in range(16):
                    sm = dsl if b % 2 == 0 else ds
                    if b >= 1:
                        s.wait_ge(sm.h, sm.n)
                    if b >= 2:
                        s.wait_ge(tm.h, t0 + 8 * (b - 2) + 8)
                    for i in range(8):
                        k = 8 * b + i
                        tc_, dc = k // NDC, k % NDC
                        sm.inc(s.dma_start(out=DST(k), in_=src_blk[dc, tc_]), 16)
                    ld_sems.append((sm.h, sm.n))

            @block.tensor
            def _(t):
                t.wait_ge(ds.h, BA_RES)
                for k in range(128):
                    smh, val = ld_sems[k // 8]
                    t.wait_ge(smh, val)
                    if k >= 2:
                        t.wait_ge(vx.h, v0 + (k - 2) + 1)
                    pb = psX if k % 2 == 0 else psAV
                    tm.inc(t.transpose(out=pb[:, 0:128], in_=DST(k), identity=identf[:, :]))

            ba_add = {}

            @block.vector
            def _(v):
                for k in range(128):
                    tc_, dc = k // NDC, k % NDC
                    v.wait_ge(tm.h, t0 + k + 1)
                    pb = psX if k % 2 == 0 else psAV
                    vx.inc(v.tensor_tensor(Bf(tc_ * 2048 + dc * 128, 128),
                                           pb[:, 0:128], Bf(tc_ * 2048 + dc * 128, 128), Alu.add))
            for k in range(128):
                ba_add[k] = v0 + k + 1

            @block.sync
            def _(s):
                for tc_ in range(NSTRIP):
                    s.wait_ge(vx.h, ba_add[tc_ * NDC + NDC - 1])
                    if tc_ >= 1:
                        s.wait_ge(dso.h, dso.n)
                    dso.inc(s.dma_start(out=dst_rows[tc_ * 128:(tc_ + 1) * 128, :],
                                        in_=Bf(tc_ * 2048, 2048)), 16)

        def emit_ffn(call):
            if call == 2:
                @block.sync
                def _(s):
                    s.wait_ge(dso.h, ST['x2t_done'])
                    for tc_ in range(NSTRIP):
                        ds.inc(s.dma_start(out=Bf(tc_ * 2048, 2048),
                                           in_=d_x2t[tc_ * 128:(tc_ + 1) * 128, :]), 16)
                ld_n = ds.n

                @block.vector
                def _(v):
                    v.wait_ge(ds.h, ld_n)
                    vx.inc(v.engine_nop())
                barrier()

            lnq_sq, lnq_vB = {}, {}
            for tc_ in range(NSTRIP):
                xs = Bf(tc_ * 2048, 2048)
                xc = Af(16384 + (tc_ % 2) * 2048, 2048)
                sq = Af(20480 + (tc_ % 2) * 2048, 2048)
                x1q = A(tc_ * 2048, 2048)

                @block.vector
                def _(v, tc_=tc_, xs=xs, xc=xc):
                    if tc_ >= 2:
                        v.wait_ge(sx.h, lnq_sq[tc_ - 2])
                    v.reduce_sum(lnm[:, tc_:tc_ + 1], xs, AX.X)
                    v.drain()
                    v.tensor_scalar_mul(lnm[:, tc_:tc_ + 1], lnm[:, tc_:tc_ + 1], 1.0 / D)
                    v.drain()
                    v.tensor_single_scalar(xc, xs, lnm[:, tc_:tc_ + 1], Alu.subtract)
                    v.drain()
                    v.reduce_max(lna[:, tc_:tc_ + 1], xc, AX.X, apply_absolute_value=True)
                    v.drain()
                    v.reciprocal(lns[:, tc_:tc_ + 1], lna[:, tc_:tc_ + 1])
                    v.drain()
                    v.tensor_scalar_mul(lns[:, tc_:tc_ + 1], lns[:, tc_:tc_ + 1], QQ)
                    v.drain()
                    vx.inc(v.engine_nop())
                vA = vx.n

                @block.scalar
                def _(s, tc_=tc_, xc=xc, sq=sq, vA=vA):
                    s.wait_ge(vx.h, vA)
                    if tc_ >= 2:
                        s.wait_ge(vx.h, lnq_vB[tc_ - 2])
                    s.drain()
                    sx.inc(s.square(out=sq, in_=xc))
                lnq_sq[tc_] = sx.n

                @block.vector
                def _(v, tc_=tc_, sq=sq):
                    v.wait_ge(sx.h, lnq_sq[tc_])
                    v.reduce_sum(lnr[:, tc_:tc_ + 1], sq, AX.X)
                    v.drain()
                    v.tensor_scalar_mul(lnr[:, tc_:tc_ + 1], lnr[:, tc_:tc_ + 1], 1.0 / D)
                    v.drain()
                    vx.inc(v.engine_nop())
                lnq_vB[tc_] = vx.n

                @block.scalar
                def _(s, tc_=tc_):
                    s.wait_ge(vx.h, lnq_vB[tc_])
                    s.drain()
                    sx.inc(s.activation(out=lnr[:, tc_:tc_ + 1], in_=lnr[:, tc_:tc_ + 1],
                                        func=Act.Sqrt, bias=epsl[:, :], scale=1.0))
                sB = sx.n

                @block.vector
                def _(v, tc_=tc_, xc=xc, x1q=x1q, sB=sB):
                    v.wait_ge(sx.h, sB)
                    v.reciprocal(lnr[:, tc_:tc_ + 1], lnr[:, tc_:tc_ + 1])
                    v.drain()
                    v.tensor_tensor(lnx[:, tc_:tc_ + 1], lna[:, tc_:tc_ + 1], lnr[:, tc_:tc_ + 1], Alu.mult)
                    v.drain()
                    v.tensor_tensor(lnx[:, tc_:tc_ + 1], lnx[:, tc_:tc_ + 1], sw1c[:, :], Alu.mult)
                    v.drain()
                    v.tensor_scalar(xc, xc, lns[:, tc_:tc_ + 1], cC[:, :], Alu.mult, Alu.add)
                    v.drain()
                    v.tensor_scalar_sub(x1q, xc, cC[:, :])
                    v.drain()
                    vx.inc(v.engine_nop())

            @block.vector
            def _(v):
                v.tensor_copy(out=p8b[:, 0:NSTRIP], in_=lnx[:, 0:NSTRIP])
                v.drain()
                vx.inc(v.engine_nop())
            vS = vx.n

            @block.tensor
            def _(t, vS=vS):
                t.wait_ge(vx.h, vS)
                tm.inc(t.transpose(out=psX[:, 0:128], in_=p8b[:, 0:128], identity=identf[:, :]))
            tS = tm.n

            @block.vector
            def _(v, tS=tS):
                v.wait_ge(tm.h, tS)
                v.tensor_copy(out=p8[:, 0:128], in_=psX[:, 0:128])
                v.drain()
                vx.inc(v.engine_nop())
            vT = vx.n

            @block.sync
            def _(s, vT=vT):
                s.wait_ge(vx.h, vT)
                dso.inc(s.dma_start(out=S1F().rearrange("a (b c) -> a b c", b=NSTRIP),
                                    in_=p8[0:NSTRIP, 0:128]), 16)
            dT = dso.n

            @block.tensor
            def _(t, dT=dT):
                t.wait_ge(dso.h, dT)
                tm.inc(t.matmul(out=psX[:, 0:512], lhsT=ones64[0:1, :], rhs=S1F()[0:1, 0:512],
                                start=True, stop=True))
                tm.inc(t.matmul(out=psS[:, 0:512], lhsT=ones64[0:1, :], rhs=S1F()[0:1, 512:1024],
                                start=True, stop=True))
            tU = tm.n

            @block.vector
            def _(v, tU=tU):
                v.wait_ge(tm.h, tU)
                v.tensor_copy(out=S1BCF1()[:, 0:512], in_=psX[:, 0:512])
                v.tensor_copy(out=S1BCF1()[:, 512:1024], in_=psS[:, 0:512])
                v.drain()
                vx.inc(v.engine_nop())

            barrier()

            t0, v0 = tm.n, vx.n

            @block.tensor
            def _(t, t0=t0, v0=v0):
                for k in range(128):
                    tc_, dc = k // NDC, k % NDC
                    if k >= 2:
                        t.wait_ge(vx.h, v0 + (k - 2) + 1)
                    pb = psTb if k % 2 == 0 else psTb2
                    tm.inc(t.transpose(out=pb[:, 0:128],
                                       in_=A(tc_ * 2048 + dc * 128, 128),
                                       identity=identb[:, :]))

            @block.vector
            def _(v, t0=t0):
                for k in range(128):
                    v.wait_ge(tm.h, t0 + k + 1)
                    tcb, dcb = k // NDC, k % NDC
                    pb = psTb if k % 2 == 0 else psTb2
                    vx.inc(v.tensor_copy(out=X1Q(dcb)[:, tcb * 128:tcb * 128 + 128],
                                         in_=pb[:, 0:128]))

            barrier()

            if DEBUG_DUMPS and call == 1:
                @block.gpsimd
                def _(g):
                    for dc in range(NDC):
                        g.wait_ge(dgo.h, dgo.n)
                        dgo.inc(g.dma_start(out=d_dbg1[dc], in_=X1Q(dc)), 16)
                barrier()

            f1_slab, f1_mm, f1_gelu, f1_chain = {}, {}, {}, {}
            f1_order = ([(4 * c_ + i_, d_w1q8A, 8 * c_ + i_) for c_ in range(8) for i_ in range(4)]
                        + [(4 * c_ + i_, d_w1q8B, 8 * c_ + 4 + i_) for c_ in range(8) for i_ in range(4)])
            for k_, (slab_ix, w1src, forig) in enumerate(f1_order):
                fc = k_
                sb_b = Bb(8192 + (fc % 3) * 2048, 2048)
                par = fc % 2

                @block.gpsimd
                def _(g, fc=fc, sb_b=sb_b, par=par, slab_ix=slab_ix, w1src=w1src):
                    sm = dg if par == 0 else dgo
                    if fc == 0:
                        g.wait_ge(cc.h, ST['ag_w1a'])
                    if fc == 32:
                        g.wait_ge(cc.h, ST['ag_w1'])
                    if fc >= 3:
                        g.wait_ge(tm.h, f1_slab[fc - 3])
                    if fc >= 2:
                        g.wait_ge(sm.h, sm.n)
                    sm.inc(g.dma_start(out=sb_b, in_=w1src[slab_ix]), 16)
                sm = dg if par == 0 else dgo
                mm_w = (sm.h, sm.n)

                ps = psS if fc % 2 == 0 else psS2
                for th in range(2):
                    @block.tensor
                    def _(t, fc=fc, th=th, sb_b=sb_b, ps=ps, mm_w=mm_w):
                        if th == 0:
                            t.wait_ge(mm_w[0], mm_w[1])
                            if fc >= 2:
                                t.wait_ge(vx.h, f1_mm[(fc - 2, 'evac')])
                        last = None
                        for c in range(NDC):
                            last = t.matmul(out=ps[:, th * 512:(th + 1) * 512],
                                            lhsT=sb_b[:, c * 128:(c + 1) * 128],
                                            rhs=X1Q(c)[:, th * 512:(th + 1) * 512],
                                            start=(c == 0), stop=(c == NDC - 1))
                        tm.inc(last)
                    f1_mm[(fc, th)] = tm.n
                    if th == 1:
                        f1_slab[fc] = tm.n

                @block.vector
                def _(v, fc=fc, ps=ps):
                    v.wait_ge(tm.h, f1_mm[(fc, 1)])
                    if fc >= 1:
                        v.wait_ge(sx.h, f1_gelu[fc - 1])
                    vx.inc(v.tensor_tensor(TT1(), ps[:, 0:1024], S1BCF1(), Alu.mult))
                f1_mm[(fc, 'evac')] = vx.n
                vB = vx.n

                @block.scalar
                def _(s, fc=fc, forig=forig, vB=vB):
                    s.wait_ge(vx.h, vB)
                    sx.inc(s.activation(out=X2(forig), in_=TT1(),
                                        func=Act.Gelu_apprx_tanh, bias=b1t[:, forig:forig + 1], scale=1.0))
                f1_gelu[fc] = sx.n

                @block.vector
                def _(v, fc=fc, forig=forig):
                    v.wait_ge(sx.h, f1_gelu[fc])
                    v.tensor_tensor(runmax[:, :], runmax[:, :], X2(forig), Alu.max)
                    v.drain()
                    v.tensor_tensor(runmin[:, :], runmin[:, :], X2(forig), Alu.min)
                    v.drain()
                    vx.inc(v.engine_nop())
                f1_chain[fc] = vx.n

            barrier()

            @block.vector
            def _(v):
                rabs = Bf(0, 1024)
                v.scalar_tensor_tensor(rabs, runmin[:, :], -1.0, runmax[:, :], Alu.mult, Alu.max)
                v.drain()
                v.memset(runmax[:, :], -3.0e38)
                v.memset(runmin[:, :], 3.0e38)
                v.drain()
                vx.inc(v.engine_nop())
            vF2 = vx.n
            tF2 = tm.n

            @block.tensor
            def _(t, vF2=vF2, tF2=tF2):
                rabs = Bf(0, 1024)
                t.wait_ge(vx.h, vF2)
                for tb in range(NSTRIP):
                    if tb >= 2:
                        t.wait_ge(vx.h, vF2 + (tb - 2) + 1)
                    pb = psX if tb % 2 == 0 else psAV
                    tm.inc(t.transpose(out=pb[:, 0:128],
                                       in_=rabs[:, tb * 128:(tb + 1) * 128], identity=identf[:, :]))

            @block.vector
            def _(v, tF2=tF2):
                for tb in range(NSTRIP):
                    v.wait_ge(tm.h, tF2 + tb + 1)
                    pb = psX if tb % 2 == 0 else psAV
                    v.reduce_max(p8[:, tb:tb + 1], pb[:, 0:128], AX.X)
                    v.drain()
                    vx.inc(v.engine_nop())
                v.tensor_single_scalar(lnx[:, 0:NSTRIP], p8[:, 0:NSTRIP], sw2c[:, :], Alu.mult)
                v.drain()
                v.tensor_scalar_mul(lnx[:, 0:NSTRIP], lnx[:, 0:NSTRIP], 0.5)
                v.drain()
                v.reciprocal(p8b[:, 0:NSTRIP], p8[:, 0:NSTRIP])
                v.drain()
                v.tensor_scalar_mul(p8b[:, 0:NSTRIP], p8b[:, 0:NSTRIP], QQ)
                v.drain()
                v.tensor_copy(out=p8b[:, 8:8 + NSTRIP], in_=lnx[:, 0:NSTRIP])
                v.drain()
                vx.inc(v.engine_nop())
            vF2b = vx.n

            @block.tensor
            def _(t, vF2b=vF2b):
                t.wait_ge(vx.h, vF2b)
                tm.inc(t.transpose(out=psX[:, 0:128], in_=p8b[:, 0:128], identity=identf[:, :]))
            tF2b = tm.n

            @block.vector
            def _(v, tF2b=tF2b):
                v.wait_ge(tm.h, tF2b)
                v.tensor_copy(out=p8[:, 0:128], in_=psX[:, 0:128])
                v.drain()
                vx.inc(v.engine_nop())
            vF2c = vx.n

            @block.sync
            def _(s, vF2c=vF2c):
                s.wait_ge(vx.h, vF2c)
                dso.inc(s.dma_start(out=S2F().rearrange("a (b c) -> a b c", b=NSTRIP),
                                    in_=p8[0:NSTRIP, 0:128]), 16)
                dso.inc(s.dma_start(out=S1F().rearrange("a (b c) -> a b c", b=NSTRIP),
                                    in_=p8[8:8 + NSTRIP, 0:128]), 16)
            dF2 = dso.n

            @block.tensor
            def _(t, dF2=dF2):
                t.wait_ge(dso.h, dF2)
                tm.inc(t.matmul(out=psX[:, 0:512], lhsT=ones64[0:1, :], rhs=S2F()[0:1, 0:512],
                                start=True, stop=True))
                tm.inc(t.matmul(out=psS[:, 0:512], lhsT=ones64[0:1, :], rhs=S2F()[0:1, 512:1024],
                                start=True, stop=True))
                tm.inc(t.matmul(out=psS2[:, 0:512], lhsT=ones64[0:1, :], rhs=S1F()[0:1, 0:512],
                                start=True, stop=True))
                tm.inc(t.matmul(out=psAV[:, 0:512], lhsT=ones64[0:1, :], rhs=S1F()[0:1, 512:1024],
                                start=True, stop=True))
            tF2c = tm.n

            @block.vector
            def _(v, tF2c=tF2c):
                v.wait_ge(tm.h, tF2c)
                v.tensor_copy(out=S2BC()[:, 0:512], in_=psX[:, 0:512])
                v.tensor_copy(out=S2BC()[:, 512:1024], in_=psS[:, 0:512])
                v.tensor_copy(out=S1BCF4()[:, 0:512], in_=psS2[:, 0:512])
                v.tensor_copy(out=S1BCF4()[:, 512:1024], in_=psAV[:, 0:512])
                v.drain()
                vx.inc(v.engine_nop())
            ST['f2_done'] = vx.n

            barrier()

            f3_v = {}
            for fc in range(NFC):
                @block.vector
                def _(v, fc=fc):
                    tt = Bf(12288 + (fc % 2) * 1024, 1024)
                    v.tensor_tensor(tt, X2(fc), S2BC()[:, :], Alu.mult)
                    v.drain()
                    vx.inc(v.tensor_scalar(X2(fc), tt, cC[:, :], cC[:, :], Alu.add, Alu.subtract))
                f3_v[fc] = vx.n
            barrier()
            if DEBUG_DUMPS and call == 1:
                @block.gpsimd
                def _(g):
                    for fc in range(NFC):
                        g.wait_ge(dgo.h, dgo.n)
                        dgo.inc(g.dma_start(out=d_dbg3[fc], in_=X2(fc)), 16)
                barrier()

            f4_slab, f4_mm, f4_res, f4_bd = {}, {}, {}, {}
            out_dst = d_x1b if call == 1 else d_x2f
            for dc in range(NDC):
                sbw = Bb((dc % 2) * 8192, 8192)
                par = dc % 2

                @block.gpsimd
                def _(g, dc=dc, sbw=sbw, par=par):
                    sm = dg if par == 0 else dgo
                    if dc == 0:
                        g.wait_ge(cc.h, ST['ag_w2'])
                    if dc < 2:
                        g.wait_ge(vx.h, ST['f2_done'])
                    if dc >= 2:
                        g.wait_ge(tm.h, f4_slab[dc - 2])
                        g.wait_ge(sm.h, sm.n)
                    sm.inc(g.dma_start(out=sbw, in_=d_w2q8[dc]), 16)
                sm = dg if par == 0 else dgo
                mm_w = (sm.h, sm.n)

                ps = psS if dc % 2 == 0 else psS2
                for th in range(2):
                    @block.tensor
                    def _(t, dc=dc, th=th, sbw=sbw, ps=ps, mm_w=mm_w):
                        if th == 0:
                            t.wait_ge(mm_w[0], mm_w[1])
                            if dc >= 2:
                                t.wait_ge(vx.h, f4_mm[(dc - 2, 'evac')])
                        last = None
                        for c in range(NFC):
                            last = t.matmul(out=ps[:, th * 512:(th + 1) * 512],
                                            lhsT=sbw[:, c * 128:(c + 1) * 128],
                                            rhs=X2(c)[:, th * 512:(th + 1) * 512],
                                            start=(c == 0), stop=(c == NFC - 1))
                        tm.inc(last)
                    f4_mm[(dc, th)] = tm.n
                    if th == 1:
                        f4_slab[dc] = tm.n

                @block.vector
                def _(v, dc=dc, ps=ps):
                    v.wait_ge(tm.h, f4_mm[(dc, 1)])
                    if dc >= 2:
                        v.wait_ge(dso.h, f4_bd[dc - 2])
                    vx.inc(v.tensor_tensor(BLK4(dc % 2), ps[:, 0:1024], S1BCF4(), Alu.mult))
                f4_mm[(dc, 'evac')] = vx.n
                vB = vx.n

                @block.scalar
                def _(s, dc=dc, vB=vB):
                    s.wait_ge(vx.h, vB)
                    sx.inc(s.activation(out=BLK4(dc % 2), in_=BLK4(dc % 2),
                                        func=Act.Identity, bias=b2h[:, dc:dc + 1], scale=1.0))
                f4_res[dc] = sx.n

                @block.sync
                def _(s, dc=dc):
                    s.wait_ge(sx.h, f4_res[dc])
                    for i in range(8):
                        if dc >= 1 or i >= 1:
                            s.wait_ge(dso.h, dso.n)
                        dso.inc(s.dma_start(out=out_dst[dc, i],
                                            in_=BLK4(dc % 2)[:, i * 128:(i + 1) * 128]), 16)
                f4_bd[dc] = dso.n

            if call == 1:
                ST['x1b_done'] = dso.n
            else:
                ST['out_done'] = dso.n
            barrier(with_queues=True)

        emit_ffn(1)

        def XN2(dc):
            return A(dc * 1024, 1024)

        def QH(h):
            return A(16384 + h * 1024, 1024)

        def APAIR(p):
            return A(16384 + 2 * p * 1024, 1024)

        def KH(kvh):
            return A(49152 + kvh * 2048, 2048)

        def VP(kc):
            return Bb(kc * 520, 520)

        def VS(vfc):
            return Bb(8320 + vfc * 2048, 2048)

        def WO(fcp):
            if fcp < 8:
                return A(fcp * 2048, 2048)
            return A(49152 + (fcp - 8) * 2048, 2048)

        def MSK(mi):
            return A(mi * 512, 512)

        emit_blkadd(1)
        barrier(with_queues=True)

        a0_ld, a0_tp, a0_ev = {}, {}, {}
        for tc_ in range(NSTRIP):
            a0_ld[tc_] = 0

        for tc_ in range(NSTRIP):
            xs = Bf(tc_ * 2048, 2048)
            sq = Af(20480 + (tc_ % 2) * 2048, 2048)
            xn2s = A(16384 + (tc_ % 2) * 2048, 2048)

            @block.vector
            def _(v, tc_=tc_, xs=xs):
                v.wait_ge(dso.h, a0_ld[tc_])
                v.reduce_sum(lnm[:, tc_:tc_ + 1], xs, AX.X)
                v.drain()
                v.tensor_scalar_mul(lnm[:, tc_:tc_ + 1], lnm[:, tc_:tc_ + 1], 1.0 / D)
                v.drain()
                vx.inc(v.engine_nop())
            vA = vx.n

            @block.scalar
            def _(s, tc_=tc_, xs=xs, sq=sq, vA=vA):
                s.wait_ge(vx.h, vA)
                if tc_ >= 2:
                    s.wait_ge(vx.h, a0_ev[(tc_ - 2, 'vB')])
                s.drain()
                sx.inc(s.activation(out=sq, in_=xs, func=Act.Square,
                                    bias=lnm[:, tc_:tc_ + 1], scale=-1.0))
            sA = sx.n

            @block.vector
            def _(v, tc_=tc_, sq=sq, sA=sA):
                v.wait_ge(sx.h, sA)
                v.reduce_sum(lnr[:, tc_:tc_ + 1], sq, AX.X)
                v.drain()
                v.tensor_scalar_mul(lnr[:, tc_:tc_ + 1], lnr[:, tc_:tc_ + 1], 1.0 / D)
                v.drain()
                vx.inc(v.engine_nop())
            a0_ev[(tc_, 'vB')] = vx.n
            vB = vx.n

            @block.scalar
            def _(s, tc_=tc_, vB=vB):
                s.wait_ge(vx.h, vB)
                s.drain()
                sx.inc(s.activation(out=lnr[:, tc_:tc_ + 1], in_=lnr[:, tc_:tc_ + 1],
                                    func=Act.Sqrt, bias=epsl[:, :], scale=1.0))
            sB = sx.n

            @block.vector
            def _(v, tc_=tc_, xs=xs, xn2s=xn2s, sB=sB):
                v.wait_ge(sx.h, sB)
                if tc_ >= 2:
                    v.wait_ge(tm.h, a0_tp[tc_ - 2])
                v.reciprocal(lnr[:, tc_:tc_ + 1], lnr[:, tc_:tc_ + 1])
                v.drain()
                v.tensor_scalar(xn2s, xs, lnm[:, tc_:tc_ + 1], lnr[:, tc_:tc_ + 1],
                                Alu.subtract, Alu.mult)
                v.drain()
                vx.inc(v.engine_nop())
            vC = vx.n

            t0 = tm.n
            v0 = vx.n

            @block.tensor
            def _(t, tc_=tc_, xn2s=xn2s, vC=vC, t0=t0, v0=v0):
                t.wait_ge(vx.h, vC)
                for dc in range(NDC):
                    if dc >= 2:
                        t.wait_ge(vx.h, v0 + (dc - 2) + 1)
                    pb = psTb if dc % 2 == 0 else psTb2
                    tm.inc(t.transpose(out=pb[:, 0:128],
                                       in_=xn2s[:, dc * 128:(dc + 1) * 128], identity=identb[:, :]))
            a0_tp[tc_] = tm.n

            @block.vector
            def _(v, tc_=tc_, t0=t0):
                for dc in range(NDC):
                    v.wait_ge(tm.h, t0 + dc + 1)
                    pb = psTb if dc % 2 == 0 else psTb2
                    vx.inc(v.tensor_copy(out=A(dc * 1024 + tc_ * 128, 128),
                                         in_=pb[:, 0:128]))
                    a0_ev[(tc_, dc)] = vx.n

        barrier(with_queues=True)

        @block.sync
        def _(s):
            ds.inc(s.dma_start(out=COSB(), in_=d_cos), 16)
            ds.inc(s.dma_start(out=SINB(), in_=d_sin), 16)
        DS_ROPE = ds.n

        @block.vector
        def _(v):
            v.wait_ge(ds.h, DS_ROPE)
            vx.inc(v.engine_nop())
        barrier()

        projs = ([('q', i) for i in range(16)] + [('k', i) for i in range(4)]
                 + [('v', i) for i in range(4)])
        WROW = {'q': 0, 'k': 2048, 'v': 2560}
        a1_lds, a1_mmv = {}, {}

        @block.gpsimd
        def _(g):
            g.wait_ge(cc.h, ST['ag_wa'])
            dg.inc(g.dma_start(out=Bb(16640, 2048), in_=d_waq8[0:128, :]), 16)
        a1_lds[0] = (dg.h, dg.n)

        a1_act, a1_rope, a1_vq, a1_pcons, a1_rcons = {}, {}, {}, {}, {}
        last_q = None
        for pi, (kind, fc) in enumerate(projs):
            bias = {'q': bqt, 'k': bkt, 'v': bvt}[kind]
            wslab = Bb(16640 + (pi % 2) * 2048, 2048)
            pstg = Bb((pi % 2) * 2048, 1024)
            prot = Bb(4096 + (pi % 2) * 1024, 1024)
            ps = psS if pi % 2 == 0 else psS2
            a1_ld = a1_lds[pi]

            for th in range(2):
                @block.tensor
                def _(t, pi=pi, th=th, wslab=wslab, ps=ps, a1_ld=a1_ld):
                    if th == 0:
                        t.wait_ge(a1_ld[0], a1_ld[1])
                        if pi >= 2:
                            t.wait_ge(sx.h, a1_act[pi - 2])
                    last = None
                    for dc in range(NDC):
                        last = t.matmul(out=ps[:, th * 512:(th + 1) * 512],
                                        lhsT=wslab[:, dc * 128:(dc + 1) * 128],
                                        rhs=XN2(dc)[:, th * 512:(th + 1) * 512],
                                        start=(dc == 0), stop=(dc == NDC - 1))
                    tm.inc(last)
            a1_mm = tm.n

            @block.scalar
            def _(s, pi=pi, fc=fc, pstg=pstg, ps=ps, bias=bias, a1_mm=a1_mm):
                s.wait_ge(tm.h, a1_mm)
                if pi >= 2:
                    hd, val = a1_pcons[pi - 2]
                    s.wait_ge(hd, val)
                sx.inc(s.activation(out=pstg[:, 0:1024], in_=ps[:, 0:1024],
                                    func=Act.Identity, bias=bias[:, fc:fc + 1], scale=swatd[:, :]))
            a1_act[pi] = sx.n

            if kind in ('q', 'k'):
                @block.tensor
                def _(t, pi=pi, pstg=pstg):
                    t.wait_ge(sx.h, a1_act[pi])
                    if a1_rope:
                        t.wait_ge(vx.h, max(a1_rope.values()))
                    tm.inc(t.matmul(out=psX[:, 0:512], lhsT=rotT[:, :], rhs=pstg[:, 0:512],
                                    start=True, stop=True))
                    tm.inc(t.matmul(out=psAV[:, 0:512], lhsT=rotT[:, :], rhs=pstg[:, 512:1024],
                                    start=True, stop=True))
                tR = tm.n

                @block.vector
                def _(v, pi=pi, pstg=pstg, prot=prot, tR=tR):
                    for th in range(2):
                        v.wait_ge(tm.h, tR - 1 + th)
                        if th == 0 and pi >= 2 and (pi - 2) in a1_rcons:
                            hd, val = a1_rcons[pi - 2]
                            v.wait_ge(hd, val)
                        psrc = psX if th == 0 else psAV
                        m1 = Bf(10368 + th * 512, 512)
                        m2 = Bf(11392 + th * 512, 512)
                        v.tensor_tensor(m1, pstg[:, th * 512:(th + 1) * 512],
                                        COSB()[:, th * 512:(th + 1) * 512], Alu.mult)
                        v.tensor_tensor(m2, psrc[:, 0:512], SINB()[:, th * 512:(th + 1) * 512], Alu.mult)
                        v.drain()
                        v.tensor_tensor(prot[:, th * 512:(th + 1) * 512], m1, m2, Alu.add)
                        v.drain()
                    vx.inc(v.engine_nop())
                a1_rope[pi] = vx.n
                ropeV = vx.n

                if kind == 'q':
                    @block.sync
                    def _(s, fc=fc, prot=prot, ropeV=ropeV):
                        s.wait_ge(vx.h, ropeV)
                        s.wait_ge(dso.h, dso.n)
                        dso.inc(s.dma_start(out=QH(2 * fc)[0:64, :], in_=prot[0:64, :]), 16)
                        dso.inc(s.dma_start(out=QH(2 * fc + 1)[0:64, :], in_=prot[64:128, :]), 16)
                    a1_rcons[pi] = (dso.h, dso.n)
                    a1_pcons[pi] = (vx.h, ropeV)
                else:
                    @block.gpsimd
                    def _(g, fc=fc, prot=prot, ropeV=ropeV):
                        g.wait_ge(vx.h, ropeV)
                        g.wait_ge(dgo.h, dgo.n)
                        dgo.inc(g.dma_start(out=d_agi[fc * 128:(fc + 1) * 128, :], in_=prot), 16)
                    a1_rcons[pi] = (dgo.h, dgo.n)
                    a1_pcons[pi] = (vx.h, ropeV)
            else:
                @block.gpsimd
                def _(g, pi=pi, fc=fc, pstg=pstg):
                    g.wait_ge(sx.h, a1_act[pi])
                    g.wait_ge(dgo.h, dgo.n)
                    dgo.inc(g.dma_start(out=d_agi[512 + fc * 128:512 + (fc + 1) * 128, :], in_=pstg), 16)
                a1_pcons[pi] = (dgo.h, dgo.n)

            a1_mmv[pi] = a1_mm
            if pi + 1 < len(projs):
                nkind, nfc = projs[pi + 1]
                row0 = WROW[nkind] + nfc * 128

                @block.gpsimd
                def _(g, pi=pi, row0=row0):
                    sm_ = dg if (pi + 1) % 2 == 0 else dga
                    if pi >= 1:
                        g.wait_ge(tm.h, a1_mmv[pi - 1])
                        g.wait_ge(sm_.h, sm_.n)
                    sm_.inc(g.dma_start(out=Bb(16640 + ((pi + 1) % 2) * 2048, 2048),
                                        in_=d_waq8[row0:row0 + 128, :]), 16)
                sm_ = dg if (pi + 1) % 2 == 0 else dga
                a1_lds[pi + 1] = (sm_.h, sm_.n)

        barrier(with_queues=True)

        @block.gpsimd
        def _(g):
            g.wait_ge(dgo.h, dgo.n)
            cc.inc(g.collective_compute("AllGather", Alu.bypass,
                                        replica_groups=AG_GROUPS,
                                        ins=[d_agi], outs=[d_ago]))
            g.wait_ge(cc.h, cc.n)
        CC_AG = cc.n

        @block.sync
        def _(s):
            for mi in range(16):
                ds.inc(s.dma_start(out=MSK(mi), in_=d_msk[mi]), 16)

        @block.vector
        def _(v):
            v.memset(Bb(0, 16 * 520), 1.0)
            v.drain()
            for kvh in range(KVH):
                v.memset(KH(kvh)[64:65, :], 1.0)
            v.drain()
            vx.inc(v.engine_nop())
        V_ONES = vx.n

        KMAP = [(0, 0, 0), (0, 512, 1536), (1, 0, 512), (1, 512, 1024)]

        @block.gpsimd
        def _(g):
            g.wait_ge(cc.h, CC_AG)
            g.wait_ge(vx.h, V_ONES)
            for kvh in range(KVH):
                for (sl, c0, k0) in KMAP:
                    dg.inc(g.dma_start(out=KH(kvh)[0:64, k0:k0 + 512],
                                       in_=d_ago[sl * 1024 + kvh * 64:sl * 1024 + (kvh + 1) * 64,
                                                 c0:c0 + 512]), 16)
        DG_K = dg.n

        @block.sync
        def _(s):
            s.wait_ge(cc.h, CC_AG)
            s.wait_ge(vx.h, V_ONES)
            for vfc in range(4):
                for (sl, c0, k0) in KMAP:
                    dso.inc(s.dma_start(out=VS(vfc)[:, k0:k0 + 512],
                                        in_=d_ago[sl * 1024 + 512 + vfc * 128:
                                                  sl * 1024 + 512 + (vfc + 1) * 128, c0:c0 + 512]), 16)
        D_VST = dso.n

        tV0 = tm.n
        vV0 = vx.n

        @block.tensor
        def _(t):
            t.wait_ge(dso.h, D_VST)
            t.wait_ge(vx.h, V_ONES)
            for vfc in range(4):
                for kc in range(16):
                    k = vfc * 16 + kc
                    if k >= 2:
                        t.wait_ge(vx.h, vV0 + (k - 2) + 1)
                    pb = psTb if k % 2 == 0 else psTb2
                    tm.inc(t.transpose(out=pb[:, 0:128],
                                       in_=VS(vfc)[:, kc * 128:(kc + 1) * 128],
                                       identity=identb[:, :]))

        @block.vector
        def _(v, tV0=tV0):
            for vfc in range(4):
                for kc in range(16):
                    k = vfc * 16 + kc
                    v.wait_ge(tm.h, tV0 + k + 1)
                    pb = psTb if k % 2 == 0 else psTb2
                    v.tensor_copy(out=VP(kc)[:, (2 * vfc) * 65:(2 * vfc) * 65 + 64],
                                  in_=pb[:, 0:64])
                    v.tensor_copy(out=VP(kc)[:, (2 * vfc + 1) * 65:(2 * vfc + 1) * 65 + 64],
                                  in_=pb[:, 64:128])
                    v.drain()
                    vx.inc(v.engine_nop())

        barrier(with_queues=True)

        @block.vector
        def _(v):
            v.memset(runmax[:, :], -3.0e38)
            v.memset(runmin[:, :], 3.0e38)
            v.drain()
            vx.inc(v.engine_nop())

        barrier(with_queues=True)

        NB = len(BATCHES)
        flat = [(h, bi, batch) for h in range(NH) for bi, batch in enumerate(BATCHES)]
        b_mm, b_exp, b_msk, b_av, evac_v = {}, {}, {}, {}, {}

        def emit_av(g):
            h, bi, batch = flat[g]
            kvh = h // 4
            ach = ATTCH(g % 4)

            @block.tensor
            def _(t, h=h, g=g, batch=batch, ach=ach, kvh=kvh):
                kd, val = b_msk[g]
                t.wait_ge(vx.h if kd == 'v' else sx.h, val)
                for i, sl in enumerate(batch):
                    qg, kc = SLOTS[sl]
                    if sl == 0 and h >= 1:
                        t.wait_ge(vx.h, evac_v[(h - 1, 1)])
                    if sl == 8:
                        t.wait_ge(vx.h, evac_v[(h, 0)])
                    tm.inc(t.matmul(out=psAV[0:65, 0:512],
                                    lhsT=VP(kc)[:, kvh * 65:(kvh + 1) * 65],
                                    rhs=ach[:, i * 512:(i + 1) * 512],
                                    start=(sl == 0 or sl == 8),
                                    stop=(sl == 7 or sl == 23),
                                    skip_group_check=True))
            b_av[g] = tm.n

            for qg_end, qg in ((7, 0), (23, 1)):
                if batch[-1] == qg_end:
                    @block.vector
                    def _(v, g=g, qg=qg):
                        v.wait_ge(tm.h, b_av[g])
                        with nc.allow_low_precision(reason="softmax denom reciprocal broadcast in bf16"):
                            vx.inc(v.reciprocal(REC64B()[64:65, :], psAV[64:65, 0:512]))
                    vR = vx.n

                    @block.tensor
                    def _(t, vR=vR):
                        t.wait_ge(vx.h, vR)
                        tm.inc(t.matmul(out=psX[:, 0:512], lhsT=onesb[64:65, 0:128],
                                        rhs=REC64B()[64:65, :], start=True, stop=True))
                    tB = tm.n

                    @block.vector
                    def _(v, h=h, qg=qg, tB=tB):
                        v.wait_ge(tm.h, tB)
                        if h % 2 == 1 and qg == 1:
                            v.wait_ge(dso.h, ST[('aodd_d', h)])
                        if h % 2 == 1 and qg == 0 and h >= 3:
                            v.wait_ge(dso.h, ST[('aodd_d2', h - 2)])
                        v.tensor_copy(out=RECB()[0:64, :], in_=psX[0:64, 0:512])
                        v.drain()
                        if h % 2 == 0:
                            v.tensor_tensor(APAIR(h // 2)[0:64, qg * 512:(qg + 1) * 512],
                                            psAV[0:64, 0:512], RECB()[0:64, :], Alu.mult)
                        else:
                            v.tensor_tensor(AODD()[0:64, :],
                                            psAV[0:64, 0:512], RECB()[0:64, :], Alu.mult)
                        v.drain()
                        vx.inc(v.engine_nop())
                    evac_v[(h, qg)] = vx.n

                    if h % 2 == 1:
                        @block.sync
                        def _(s, h=h, qg=qg):
                            s.wait_ge(vx.h, evac_v[(h, qg)])
                            if qg == 1:
                                s.wait_ge(dso.h, dso.n)
                            dso.inc(s.dma_start(out=APAIR(h // 2)[64:128, qg * 512:(qg + 1) * 512],
                                                in_=AODD()[0:64, :]), 16)
                        if qg == 0:
                            ST[('aodd_d', h)] = dso.n
                        else:
                            ST[('aodd_d2', h)] = dso.n

        for g, (h, bi, batch) in enumerate(flat):
            kvh = h // 4
            psb = psS if g % 2 == 0 else psS2
            ach = ATTCH(g % 4)

            @block.tensor
            def _(t, h=h, g=g, batch=batch, psb=psb, kvh=kvh):
                if g >= 2:
                    t.wait_ge(sx.h, b_exp[g - 2])
                for i, sl in enumerate(batch):
                    qg, kc = SLOTS[sl]
                    tm.inc(t.matmul(out=psb[:, i * 512:(i + 1) * 512],
                                    lhsT=KH(kvh)[0:64, kc * 128:(kc + 1) * 128],
                                    rhs=QH(h)[0:64, qg * 512:(qg + 1) * 512],
                                    start=True, stop=True))
            b_mm[g] = tm.n

            @block.scalar
            def _(s, g=g, batch=batch, psb=psb, ach=ach):
                s.wait_ge(tm.h, b_mm[g])
                if g >= 4:
                    s.wait_ge(tm.h, b_av[g - 4])
                n = len(batch) * 512
                sx.inc(s.activation(out=ach[:, 0:n], in_=psb[:, 0:n], func=Act.Exp,
                                    bias=0.0, scale=0.125))
            b_exp[g] = sx.n

            if any(sl in MASKIDX for sl in batch):
                @block.vector
                def _(v, g=g, batch=batch, ach=ach):
                    v.wait_ge(sx.h, b_exp[g])
                    for i, sl in enumerate(batch):
                        if sl in MASKIDX:
                            v.tensor_tensor(ach[:, i * 512:(i + 1) * 512],
                                            ach[:, i * 512:(i + 1) * 512],
                                            MSK(MASKIDX[sl]), Alu.mult)
                    v.drain()
                    vx.inc(v.engine_nop())
                b_msk[g] = ('v', vx.n)
            else:
                b_msk[g] = ('s', b_exp[g])

            if g >= 1:
                emit_av(g - 1)
        emit_av(len(flat) - 1)

        barrier(with_queues=True)

        @block.sync
        def _(s):
            ds.inc(s.dma_start(out=BOROW(), in_=d_bo), 16)
        DS_BO = ds.n

        @block.vector
        def _(v):
            v.wait_ge(ds.h, DS_BO)
            vx.inc(v.engine_nop())
        barrier()

        for i in range(4):
            @block.tensor
            def _(t, i=i):
                if i >= 1:
                    t.wait_ge(vx.h, ST[('bobc', i - 1)])
                tm.inc(t.matmul(out=psX[:, 0:512], lhsT=ones64[0:1, :],
                                rhs=BOROW()[0:1, i * 512:(i + 1) * 512], start=True, stop=True))
            tO = tm.n

            @block.vector
            def _(v, i=i, tO=tO):
                v.wait_ge(tm.h, tO)
                v.tensor_copy(out=BOBC()[:, i * 512:(i + 1) * 512], in_=psX[:, 0:512])
                v.drain()
                vx.inc(v.engine_nop())
            ST[('bobc', i)] = vx.n
        V_BOBC = vx.n

        wo_ld = {}
        for fcp in range(NDC):
            @block.gpsimd
            def _(g, fcp=fcp):
                if fcp == 0:
                    g.wait_ge(vx.h, V_BOBC)
                sm_ = dg if fcp % 2 == 0 else dga
                if fcp >= 2:
                    g.wait_ge(sm_.h, sm_.n)
                r0 = 3072 + fcp * 128
                sm_.inc(g.dma_start(out=WO(fcp), in_=d_waq8[r0:r0 + 128, :]), 16)
            sm_ = dg if fcp % 2 == 0 else dga
            wo_ld[fcp] = (sm_.h, sm_.n)

        wo_sc = {}
        for fcp in range(NDC):
            @block.vector
            def _(v, fcp=fcp):
                v.wait_ge(wo_ld[fcp][0], wo_ld[fcp][1])
                with nc.allow_low_precision(reason="int8 Wo dequant scale in bf16"):
                    vx.inc(v.tensor_single_scalar(WO(fcp), WO(fcp), swatd[:, 0:1], Alu.mult))
            wo_sc[fcp] = vx.n
        DG_WO = None

        barrier(with_queues=True)

        a3_mm, a3_v, a3_bd = {}, {}, {}
        for tc_ in range(NSTRIP):
            x1sl = Bf(8192 + (tc_ % 2) * 2048, 2048)
            x2sl = Bf(12288 + (tc_ % 2) * 2048, 2048)

            @block.sync
            def _(s, tc_=tc_, x1sl=x1sl):
                if tc_ >= 2:
                    s.wait_ge(vx.h, a3_v[(tc_ - 2, 3)])
                if tc_ >= 1:
                    s.wait_ge(dso.h, dso.n)
                dso.inc(s.dma_start(out=x1sl, in_=d_x1t[tc_ * 128:(tc_ + 1) * 128, :]), 16)
            a3_x1 = dso.n

            for dh in range(4):
                ps = psS if dh % 2 == 0 else psS2
                @block.tensor
                def _(t, tc_=tc_, dh=dh, ps=ps):
                    if dh >= 2:
                        t.wait_ge(vx.h, a3_v[(tc_, dh - 2)])
                    elif tc_ >= 1:
                        t.wait_ge(vx.h, a3_v[(tc_ - 1, dh + 2)])
                    last = None
                    for fcp in range(NDC):
                        last = t.matmul(out=ps[:, 0:512], lhsT=APAIR(fcp)[:, tc_ * 128:(tc_ + 1) * 128],
                                        rhs=WO(fcp)[:, dh * 512:(dh + 1) * 512],
                                        start=(fcp == 0), stop=(fcp == NDC - 1))
                    tm.inc(last)
                a3_mm[(tc_, dh)] = tm.n

                @block.vector
                def _(v, tc_=tc_, dh=dh, ps=ps, x1sl=x1sl, x2sl=x2sl):
                    v.wait_ge(tm.h, a3_mm[(tc_, dh)])
                    v.wait_ge(dso.h, a3_x1)
                    if tc_ >= 2:
                        v.wait_ge(dso.h, a3_bd[(tc_ - 2, 'x2t')])
                    v.scalar_tensor_tensor(x2sl[:, dh * 512:(dh + 1) * 512], ps[:, 0:512], 1.0,
                                           BOBC()[:, dh * 512:(dh + 1) * 512], Alu.mult, Alu.add)
                    v.drain()
                    v.tensor_tensor(x2sl[:, dh * 512:(dh + 1) * 512],
                                    x2sl[:, dh * 512:(dh + 1) * 512],
                                    x1sl[:, dh * 512:(dh + 1) * 512], Alu.add)
                    v.drain()
                    vx.inc(v.engine_nop())
                a3_v[(tc_, dh)] = vx.n

            @block.sync
            def _(s, tc_=tc_, x2sl=x2sl):
                for dh in range(4):
                    s.wait_ge(vx.h, a3_v[(tc_, dh)])
                dso.inc(s.dma_start(out=d_x2t[tc_ * 128:(tc_ + 1) * 128, :], in_=x2sl), 16)
            a3_bd[(tc_, 'x2t')] = dso.n

        ST['x2t_done'] = max(a3_bd[(tc_, 'x2t')] for tc_ in range(NSTRIP))
        barrier(with_queues=True)

        emit_ffn(2)

        emit_blkadd(2)
        barrier(with_queues=True)

    return nc


def _host_prep(inputs):
    _CACHE = {}
    W1 = np.ascontiguousarray(inputs["L0_ffn_W1"], dtype=np.float32)
    W2 = np.ascontiguousarray(inputs["L0_ffn_W2"], dtype=np.float32)
    Wq = np.ascontiguousarray(inputs["L0_attn_Wq"], dtype=np.float32)
    Wk = np.ascontiguousarray(inputs["L0_attn_Wk"], dtype=np.float32)
    Wv = np.ascontiguousarray(inputs["L0_attn_Wv"], dtype=np.float32)
    Wo = np.ascontiguousarray(inputs["L0_attn_Wo"], dtype=np.float32)
    C = _CACHE
    C['w1b'] = np.ascontiguousarray(W1.reshape(64, 128, 16, 128).transpose(0, 3, 2, 1)).reshape(64, 128, 2048)
    C['w2b'] = np.ascontiguousarray(W2.reshape(16, 128, 64, 128).transpose(0, 3, 2, 1)).reshape(16, 128, 8192)
    C['wqb'] = np.ascontiguousarray(Wq.reshape(16, 128, 16, 128).transpose(2, 1, 0, 3)).reshape(16, 128, 2048)
    C['wkb'] = np.ascontiguousarray(Wk.reshape(16, 128, 4, 128).transpose(2, 1, 0, 3)).reshape(4, 128, 2048)
    C['wvb'] = np.ascontiguousarray(Wv.reshape(16, 128, 4, 128).transpose(2, 1, 0, 3)).reshape(4, 128, 2048)
    C['wob'] = np.ascontiguousarray(Wo.reshape(16, 128, 2048))
    C['waflat'] = np.ascontiguousarray(np.concatenate(
        [C['wqb'].reshape(2048, 2048), C['wkb'].reshape(512, 2048),
         C['wvb'].reshape(512, 2048), C['wob'].reshape(2048, 2048)], axis=0))
    C['b1t'] = np.ascontiguousarray(inputs["L0_ffn_B1"].reshape(64, 128).T, dtype=np.float32)
    C['b2t'] = np.ascontiguousarray(inputs["L0_ffn_B2"].reshape(16, 128).T, dtype=np.float32)
    C['bqt'] = np.ascontiguousarray(inputs["L0_attn_bq"].reshape(16, 128).T, dtype=np.float32)
    C['bkt'] = np.ascontiguousarray(inputs["L0_attn_bk"].reshape(4, 128).T, dtype=np.float32)
    C['bvt'] = np.ascontiguousarray(inputs["L0_attn_bv"].reshape(4, 128).T, dtype=np.float32)
    C['bor'] = np.ascontiguousarray(inputs["L0_attn_bo"].reshape(1, 2048), dtype=np.float32)
    C['idf'] = np.eye(128, dtype=np.float32)
    C['idb'] = np.eye(128, dtype=BFD)
    P = np.zeros((128, 128), dtype=np.float32)
    for i in range(64):
        P[2 * i, 2 * i + 1] = -1.0
        P[2 * i + 1, 2 * i] = 1.0
    C['rotT'] = np.ascontiguousarray(P.T, dtype=BFD)
    inv = (1.0 / (10000.0 ** (np.arange(0, HD, 2, dtype=np.float32) / HD))).astype(np.float32)
    cosd = {}
    for j in (0, 1):
        pos = _pos_list(j).astype(np.float32)
        ang = np.outer(inv.repeat(2), pos)
        cosv = np.cos(ang).astype(np.float32)
        sinv = np.sin(ang).astype(np.float32)
        cosd[j] = (np.tile(cosv, (2, 1)).astype(BFD), np.tile(sinv, (2, 1)).astype(BFD))
    C['rope'] = cosd
    msks = {}
    for j in (0, 1):
        pos = _pos_list(j)
        tiles = np.zeros((16, 128, 512), dtype=BFD)
        for mi, sl in enumerate(MASKED):
            qg, kc = SLOTS[sl]
            q = pos[qg * 512:(qg + 1) * 512][None, :]
            k = (np.arange(128) + kc * 128)[:, None]
            tiles[mi] = (q >= k).astype(np.float32).astype(BFD)
        msks[j] = tiles
    C['msk'] = msks
    return C


def kernel(**inputs):
    C = _host_prep(inputs)
    x = np.ascontiguousarray(inputs["L0_attn_x"], dtype=np.float32)
    nc = build_program()
    in_maps = []
    for c in range(8):
        b, j = c // 2, c % 2
        pos = _pos_list(j)
        m = dict(
            xg=np.ascontiguousarray(x[b][pos]),
            w1s=np.ascontiguousarray(C['w1b'][8 * c:8 * c + 8]),
            w2s=np.ascontiguousarray(C['w2b'][2 * c:2 * c + 2]),
            was=np.ascontiguousarray(C['waflat'][640 * c:640 * (c + 1)].reshape(5, 128, D)),
            b1t=C['b1t'], b2t=C['b2t'], bqt=C['bqt'], bkt=C['bkt'], bvt=C['bvt'], bor=C['bor'],
            cosb=C['rope'][j][0], sinb=C['rope'][j][1],
            idf=C['idf'], idb=C['idb'], rotT=C['rotT'], msk=C['msk'][j],
        )
        in_maps.append(m)
    res = run_bass_kernel_spmd(nc, in_maps, core_ids=list(range(8)))
    out = np.empty((B, T, D), dtype=np.float32)
    for c in range(8):
        b, j = c // 2, c % 2
        pos = _pos_list(j)
        out[b][pos] = np.asarray(res.results[c]["outb"])
    return out


# version 19
# speedup vs baseline: 1.0217x; 1.0217x over previous
import os
import numpy as np
import ml_dtypes
from contextlib import ExitStack

import concourse.bass as bass
import concourse.mybir as mybir
from concourse.bass_utils import run_bass_kernel_spmd

F32 = mybir.dt.float32
BF16 = mybir.dt.bfloat16
I8 = mybir.dt.int8
Alu = mybir.AluOpType
Act = mybir.ActivationFunctionType
AX = mybir.AxisListType
BFD = ml_dtypes.bfloat16

B, T, D, HFF = 4, 2048, 2048, 8192
NH, KVH, HD = 32, 8, 64
TC = 1024
NSTRIP = 8
NDC = 16
NFC = 64
QQ = 127.0
C32 = float(1.5 * 2.0 ** 23)
C64 = float(1.5 * 2.0 ** 52)

SLOTS = [(0, s) for s in range(8)] + [(1, s - 8) for s in range(8, 24)]
MASKED = sorted(set(range(0, 8)) | set(range(16, 24)))
MASKIDX = {s: i for i, s in enumerate(MASKED)}
BATCHES = [[0, 1, 2], [3, 4, 5], [6, 7], [8, 9, 10], [11, 12, 13], [14, 15, 16],
           [17, 18, 19], [20, 21, 22], [23]]

AR_GROUPS = [list(range(8))]
DEBUG_DUMPS = False
AG_GROUPS = [[0, 1], [2, 3], [4, 5], [6, 7]]


def _pos_list(j):
    g1 = np.arange(512 * j, 512 * (j + 1))
    g2 = np.arange(2048 - 512 * (j + 1), 2048 - 512 * j)
    return np.concatenate([g1, g2])


class Sem:
    def __init__(self, E, nc, name):
        self.h = E(nc.semaphore(name))
        self.n = 0

    def inc(self, instr, k=1):
        instr.then_inc(self.h, k)
        self.n += k
        return self.n


def build_program():
    nc = bass.Bass("TRN2", target_bir_lowering=False)

    d_xg = nc.dram_tensor("xg", [TC, D], F32, kind="ExternalInput").ap()
    d_w1s = nc.dram_tensor("w1s", [8, 128, D], F32, kind="ExternalInput").ap()
    d_w2s = nc.dram_tensor("w2s", [2, 128, HFF], F32, kind="ExternalInput").ap()
    d_was = nc.dram_tensor("was", [5, 128, D], F32, kind="ExternalInput").ap()
    d_b1 = nc.dram_tensor("b1t", [128, NFC], F32, kind="ExternalInput").ap()
    d_b2 = nc.dram_tensor("b2t", [128, NDC], F32, kind="ExternalInput").ap()
    d_bq = nc.dram_tensor("bqt", [128, 16], F32, kind="ExternalInput").ap()
    d_bk = nc.dram_tensor("bkt", [128, 4], F32, kind="ExternalInput").ap()
    d_bv = nc.dram_tensor("bvt", [128, 4], F32, kind="ExternalInput").ap()
    d_bo = nc.dram_tensor("bor", [1, D], F32, kind="ExternalInput").ap()
    d_cos = nc.dram_tensor("cosb", [128, TC], BF16, kind="ExternalInput").ap()
    d_sin = nc.dram_tensor("sinb", [128, TC], BF16, kind="ExternalInput").ap()
    d_idf = nc.dram_tensor("idf", [128, 128], F32, kind="ExternalInput").ap()
    d_idb = nc.dram_tensor("idb", [128, 128], BF16, kind="ExternalInput").ap()
    d_rot = nc.dram_tensor("rotT", [128, 128], BF16, kind="ExternalInput").ap()
    d_msk = nc.dram_tensor("msk", [16, 128, 512], BF16, kind="ExternalInput").ap()

    d_out = nc.dram_tensor("outb", [TC, D], F32, kind="ExternalOutput").ap()
    d_waqs = nc.dram_tensor("waqs", [5, 128, D], I8, kind="Internal").ap()
    d_waq8 = nc.dram_tensor("waq8", [5120, D], I8, kind="Internal").ap()
    d_w1qsA = nc.dram_tensor("w1qsA", [4, 128, D], I8, kind="Internal").ap()
    d_w1qsB = nc.dram_tensor("w1qsB", [4, 128, D], I8, kind="Internal").ap()
    d_w2qs = nc.dram_tensor("w2qs", [2, 128, HFF], I8, kind="Internal").ap()
    d_w1q8A = nc.dram_tensor("w1q8A", [32, 128, D], I8, kind="Internal").ap()
    d_w1q8B = nc.dram_tensor("w1q8B", [32, 128, D], I8, kind="Internal").ap()
    d_w2q8 = nc.dram_tensor("w2q8", [NDC, 128, HFF], I8, kind="Internal").ap()
    d_x1b = nc.dram_tensor("x1b", [NDC, NSTRIP, 128, 128], F32, kind="Internal").ap()
    d_x2f = nc.dram_tensor("x2f", [NDC, NSTRIP, 128, 128], F32, kind="Internal").ap()
    d_x1t = nc.dram_tensor("x1t", [TC, D], F32, kind="Internal").ap()
    d_x2t = nc.dram_tensor("x2t", [TC, D], F32, kind="Internal").ap()
    d_agi = nc.dram_tensor("agi", [TC, TC], BF16, kind="Internal").ap()
    d_ago = nc.dram_tensor("ago", [2 * TC, TC], BF16, kind="Internal").ap()
    d_ari = nc.dram_tensor("ari", [1, 4], F32, kind="Internal").ap()
    d_aro = nc.dram_tensor("aro", [1, 4], F32, kind="Internal").ap()
    d_dbg1 = nc.dram_tensor("dbgx1q", [NDC, 128, TC], BF16, kind="Internal").ap() if DEBUG_DUMPS else None
    d_dbg2 = nc.dram_tensor("dbgx2", [NFC, 128, TC], BF16, kind="Internal").ap() if DEBUG_DUMPS else None
    d_dbg3 = nc.dram_tensor("dbgx2q", [NFC, 128, TC], BF16, kind="Internal").ap() if DEBUG_DUMPS else None

    ctx = ExitStack()
    E = ctx.enter_context
    with ctx:
        arA = E(nc.sbuf_tensor("arA", [128, 65536], BF16))
        arB = E(nc.sbuf_tensor("arB", [128, 16384], F32))
        _A, _B = arA.ap(), arB.ap()
        arAf = _A.bitcast(F32)
        arBb = _B.bitcast(BF16)

        def A(c0, n):
            return _A[:, c0:c0 + n]

        def Af(c0, n):
            return arAf[:, c0:c0 + n]

        def Bf(c0, n):
            return _B[:, c0:c0 + n]

        def Bb(c0, n):
            return arBb[:, c0:c0 + n]

        def COSB():
            return Bb(6144, 1024)

        def SINB():
            return Bb(7168, 1024)

        def QN2P():
            return Bf(13440, 1024)

        def MRQ():
            return Bb(28928, 1024)

        def ATTCH(sl4):
            return Bb(16384 + sl4 * 1536, 1536)

        def REC64():
            return Bf(5696, 512)

        def AODD():
            return Bb(12416, 512)

        def RECB():
            return Bf(6464, 512)

        def REC64B():
            return Bb(29952, 512)

        def BOBC():
            return Bf(0, 2048)

        def BOROW():
            return Bf(2048, 2048)[0:1, :]

        def S1F():
            return Bf(1024, 1024)[0:1, :]

        def S2F():
            return Bf(2048, 1024)[0:1, :]

        def QROW1():
            return Af(24576, 1024)[0:1, :]

        def QROW2():
            return Af(25600, 1024)[0:1, :]

        def S1BCF1():
            return Bb(15360, 1024)

        def TT1():
            return Bb(14336, 1024)

        def S1BCF4():
            return Bb(28672, 1024)

        def BLK4(sl):
            return Bf(12288 + sl * 1024, 1024)

        def S2BC():
            return Bf(8192, 1024)

        identf = E(nc.sbuf_tensor("identf", [128, 128], F32))
        identb = E(nc.sbuf_tensor("identb", [128, 128], BF16))
        rotT = E(nc.sbuf_tensor("rotTt", [128, 128], BF16))
        b1t = E(nc.sbuf_tensor("b1tt", [128, NFC], F32))
        b2h = E(nc.sbuf_tensor("b2ht", [128, NDC], F32))
        bqt = E(nc.sbuf_tensor("bqtt", [128, 16], F32))
        bkt = E(nc.sbuf_tensor("bktt", [128, 4], F32))
        bvt = E(nc.sbuf_tensor("bvtt", [128, 4], F32))
        onesc = E(nc.sbuf_tensor("onesc", [128, 1], F32))
        ones64 = E(nc.sbuf_tensor("ones64", [128, 128], F32))
        cC = E(nc.sbuf_tensor("cCt", [128, 1], F32))
        epsl = E(nc.sbuf_tensor("epsl", [128, 1], F32))
        sw1s = E(nc.sbuf_tensor("sw1s", [128, 1], F32))
        sw2s = E(nc.sbuf_tensor("sw2s", [128, 1], F32))
        sw1c = E(nc.sbuf_tensor("sw1c", [128, 1], F32))
        sw2c = E(nc.sbuf_tensor("sw2c", [128, 1], F32))
        swats = E(nc.sbuf_tensor("swats", [128, 1], F32))
        swatd = E(nc.sbuf_tensor("swatd", [128, 1], F32))
        runmax = E(nc.sbuf_tensor("runmax", [128, TC], BF16))
        runmin = E(nc.sbuf_tensor("runmin", [128, TC], BF16))
        p8 = E(nc.sbuf_tensor("p8", [128, 128], F32))
        p8b = E(nc.sbuf_tensor("p8b", [128, 128], F32))
        lnm = E(nc.sbuf_tensor("lnm", [128, NSTRIP], F32))
        lnr = E(nc.sbuf_tensor("lnr", [128, NSTRIP], F32))
        lna = E(nc.sbuf_tensor("lna", [128, NSTRIP], F32))
        lns = E(nc.sbuf_tensor("lns", [128, NSTRIP], F32))
        lnx = E(nc.sbuf_tensor("lnx", [128, NSTRIP], F32))
        wabs = E(nc.sbuf_tensor("wabs", [1, 4], F32))
        armx = E(nc.sbuf_tensor("armx", [1, 4], F32))
        blk = E(nc.sbuf_tensor("blk", [128, 1024], F32))
        onesb = E(nc.sbuf_tensor("onesb", [128, 128], BF16))

        psS = E(nc.psum_tensor("psS", [128, 1536], F32))
        psS2 = E(nc.psum_tensor("psS2", [128, 1536], F32))
        psAV = E(nc.psum_tensor("psAV", [128, 512], F32))
        psX = E(nc.psum_tensor("psX", [128, 512], F32))
        psTb = psS.ap().bitcast(BF16)[:, 2048:3072]
        psTb2 = psS2.ap().bitcast(BF16)[:, 2048:3072]

        sms = {n: Sem(E, nc, n) for n in
               ("ds", "dsl", "dso", "dg", "dgo", "dga", "tm", "vx", "sx", "cc",
                "vq0", "vq1", "cq0", "cq1", "tq0", "tq1")}
        ds, dsl, dso = sms["ds"], sms["dsl"], sms["dso"]
        dg, dgo = sms["dg"], sms["dgo"]
        tm, vx, sx, cc = sms["tm"], sms["vx"], sms["sx"], sms["cc"]
        dga = sms["dga"]
        vq = (sms["vq0"], sms["vq1"])
        cq = (sms["cq0"], sms["cq1"])
        tq = (sms["tq0"], sms["tq1"])

        block = E(nc.Block())

        def barrier(with_queues=True):
            t_n, v_n, s_n = tm.n + 1, vx.n + 1, sx.n + 1
            if isinstance(with_queues, tuple):
                q = {n: sms[n].n for n in with_queues}
                qsel = [(sms[n], q[n]) for n in with_queues]
                with_queues = 'sel'
            else:
                q = dict(ds=ds.n, dsl=dsl.n, dso=dso.n, dg=dg.n, dgo=dgo.n)

            @block.tensor
            def _(t):
                t.drain()
                tm.inc(t.nop())
                t.wait_ge(vx.h, v_n)
                t.wait_ge(sx.h, s_n)
                if with_queues == 'sel':
                    for s_, v_ in qsel:
                        t.wait_ge(s_.h, v_)
                elif with_queues:
                    for s_, v_ in ((ds, q['ds']), (dsl, q['dsl']), (dso, q['dso']),
                                   (dg, q['dg']), (dgo, q['dgo'])):
                        t.wait_ge(s_.h, v_)

            @block.vector
            def _(v):
                v.drain()
                vx.inc(v.engine_nop())
                v.wait_ge(tm.h, t_n)
                v.wait_ge(sx.h, s_n)
                if with_queues == 'sel':
                    for s_, v_ in qsel:
                        v.wait_ge(s_.h, v_)
                elif with_queues:
                    for s_, v_ in ((ds, q['ds']), (dsl, q['dsl']), (dso, q['dso']),
                                   (dg, q['dg']), (dgo, q['dgo'])):
                        v.wait_ge(s_.h, v_)

            @block.scalar
            def _(s):
                s.drain()
                sx.inc(s.nop())
                s.wait_ge(tm.h, t_n)
                s.wait_ge(vx.h, v_n)
                if with_queues == 'sel':
                    for s_, v_ in qsel:
                        s.wait_ge(s_.h, v_)
                elif with_queues:
                    for s_, v_ in ((ds, q['ds']), (dsl, q['dsl']), (dso, q['dso']),
                                   (dg, q['dg']), (dgo, q['dgo'])):
                        s.wait_ge(s_.h, v_)

            @block.sync
            def _(s):
                s.wait_ge(tm.h, t_n)
                s.wait_ge(vx.h, v_n)
                s.wait_ge(sx.h, s_n)
                if with_queues == 'sel':
                    for s_, v_ in qsel:
                        s.wait_ge(s_.h, v_)
                elif with_queues:
                    for s_, v_ in ((dsl, q['dsl']), (dso, q['dso']),
                                   (dg, q['dg']), (dgo, q['dgo'])):
                        s.wait_ge(s_.h, v_)

            @block.gpsimd
            def _(g):
                g.wait_ge(tm.h, t_n)
                g.wait_ge(vx.h, v_n)
                g.wait_ge(sx.h, s_n)
                if with_queues == 'sel':
                    for s_, v_ in qsel:
                        g.wait_ge(s_.h, v_)
                elif with_queues:
                    for s_, v_ in ((ds, q['ds']), (dsl, q['dsl']), (dso, q['dso'])):
                        g.wait_ge(s_.h, v_)

        ST = {}

        def X1Q(dc):
            return Bb(16384 + dc * 1024, 1024)

        def X2(fc):
            return A(fc * 1024, 1024)

        @block.sync
        def _(s):
            for tc_ in range(NSTRIP):
                ds.inc(s.dma_start(out=Bf(tc_ * 2048, 2048), in_=d_xg[tc_ * 128:(tc_ + 1) * 128, :]), 16)
            for dst, src in ((identf[:, :], d_idf), (identb[:, :], d_idb), (rotT[:, :], d_rot),
                             (b1t[:, :], d_b1), (b2h[:, :], d_b2), (bqt[:, :], d_bq),
                             (bkt[:, :], d_bk), (bvt[:, :], d_bv)):
                ds.inc(s.dma_start(out=dst, in_=src), 16)
        P0_LOADS = ds.n

        @block.vector
        def _(v):
            v.memset(cC[:, :], 2.0 ** -30)
            v.memset(epsl[:, :], 1e-5)
            v.memset(onesc[:, :], 1.0)
            v.memset(ones64[:, :], 1.0)
            v.memset(onesb[:, :], 1.0)
            v.memset(runmax[:, :], -3.0e38)
            v.memset(runmin[:, :], 3.0e38)
            v.memset(p8[:, :], 0.0)
            v.memset(p8b[:, :], 0.0)
            v.drain()
            v.tensor_scalar(cC[:, :], cC[:, :], 1.0, 1.0, Alu.add, Alu.subtract)
            v.drain()
            v.tensor_scalar(cC[:, :], cC[:, :], float((C64 - C32) * (2.0 ** 30)), C32, Alu.mult, Alu.add)
            v.drain()
            v.wait_ge(ds.h, P0_LOADS)
            v.tensor_scalar_mul(b2h[:, :], b2h[:, :], 0.5)
            v.drain()
            vx.inc(v.engine_nop())
        V_CONST = vx.n

        @block.gpsimd
        def _(g):
            for i in range(5):
                dg.inc(g.dma_start(out=Af(i * 2048, 2048), in_=d_was[i]), 16)
        DG_WA = dg.n

        @block.sync
        def _(s):
            for i in range(8):
                dsl.inc(s.dma_start(out=Af(10240 + i * 2048, 2048), in_=d_w1s[i]), 16)
        DSL_W1 = dsl.n

        w2ld, w2v = {}, {}
        for k in range(8):
            @block.scalar
            def _(s, k=k):
                sm_ = cq[k % 2]
                if k >= 2:
                    s.wait_ge(vx.h, w2v[k - 2])
                    s.wait_ge(sm_.h, sm_.n)
                sm_.inc(s.dma_start(out=Af(26624 + (k % 2) * 2048, 2048),
                                    in_=d_w2s[k // 4][:, (k % 4) * 2048:(k % 4 + 1) * 2048]), 16)
            w2ld[k] = (cq[k % 2].h, cq[k % 2].n)

            @block.vector
            def _(v, k=k):
                v.wait_ge(w2ld[k][0], w2ld[k][1])
                if k == 0:
                    v.memset(psS[:, 0:512], -3.0e38)
                    v.memset(psS2[:, 0:512], 3.0e38)
                    v.drain()
                for c in range(4):
                    v.tensor_tensor(psS[:, 0:512], psS[:, 0:512],
                                    Af(26624 + (k % 2) * 2048 + c * 512, 512), Alu.max)
                    v.drain()
                    v.tensor_tensor(psS2[:, 0:512], psS2[:, 0:512],
                                    Af(26624 + (k % 2) * 2048 + c * 512, 512), Alu.min)
                    v.drain()
                vx.inc(v.engine_nop())
            w2v[k] = vx.n

        @block.vector
        def _(v):
            def chain(base, n, col, pmx, pmn):
                v.memset(pmx[:, 0:512], -3.0e38)
                v.memset(pmn[:, 0:512], 3.0e38)
                v.drain()
                for k in range(n):
                    v.tensor_tensor(pmx[:, 0:512], pmx[:, 0:512], Af(base + k * 512, 512), Alu.max)
                    v.drain()
                    v.tensor_tensor(pmn[:, 0:512], pmn[:, 0:512], Af(base + k * 512, 512), Alu.min)
                    v.drain()

            def absred(dst_ap, pmx, pmn):
                v.tensor_copy(out=blk[:, 512:1024], in_=pmn[:, 0:512])
                v.drain()
                v.scalar_tensor_tensor(blk[:, 0:512], blk[:, 512:1024], -1.0, pmx[:, 0:512],
                                       Alu.mult, Alu.max)
                v.drain()
                v.reduce_max(dst_ap, blk[:, 0:512], AX.X)
                v.drain()

            v.wait_ge(dsl.h, DSL_W1)
            chain(10240, 32, 0, psX, psAV)
            absred(p8[:, 0:1], psX, psAV)
            v.wait_ge(dg.h, DG_WA)
            chain(0, 20, 2, psX, psAV)
            absred(p8b[:, 1:2], psX, psAV)
            v.wait_ge(vx.h, w2v[7])
            absred(p8b[:, 0:1], psS, psS2)
            v.drain()
            vx.inc(v.engine_nop())
        V_WABS0 = vx.n

        @block.tensor
        def _(t):
            t.wait_ge(vx.h, V_WABS0)
            t.wait_ge(ds.h, P0_LOADS)
            tm.inc(t.transpose(out=psX[:, 0:128], in_=p8[:, 0:128], identity=identf[:, :]))
            tm.inc(t.transpose(out=psX[:, 128:256], in_=p8b[:, 0:128], identity=identf[:, :]))
        T_WTP = tm.n

        @block.vector
        def _(v):
            v.wait_ge(tm.h, T_WTP)
            v.reduce_max(wabs[0:1, 0:1], psX[0:1, 0:128], AX.X)
            v.reduce_max(wabs[0:1, 1:2], psX[0:1, 128:256], AX.X)
            v.drain()
            v.memset(wabs[0:1, 2:4], 0.0)
            v.drain()
            v.tensor_copy(out=p8[:, 0:1], in_=p8b[:, 1:2])
            v.drain()
            vx.inc(v.engine_nop())
        V_WABS1 = vx.n

        @block.tensor
        def _(t):
            t.wait_ge(vx.h, V_WABS1)
            tm.inc(t.transpose(out=psX[:, 256:384], in_=p8[:, 0:128], identity=identf[:, :]))
        T_WTP2 = tm.n

        @block.vector
        def _(v):
            v.wait_ge(tm.h, T_WTP2)
            v.reduce_max(wabs[0:1, 2:3], psX[0:1, 256:384], AX.X)
            v.drain()
            vx.inc(v.engine_nop())
        V_WABS = vx.n

        @block.gpsimd
        def _(g):
            g.wait_ge(vx.h, V_WABS)
            dg.inc(g.dma_start(out=d_ari, in_=wabs[0:1, :]), 16)
            g.wait_ge(dg.h, dg.n)
            cc.inc(g.collective_compute("AllReduce", Alu.max, replica_groups=AR_GROUPS,
                                        ins=[d_ari], outs=[d_aro]))
            g.wait_ge(cc.h, cc.n)
            dg.inc(g.dma_start(out=armx[0:1, :], in_=d_aro), 16)
        G_ARMX = dg.n

        @block.vector
        def _(v):
            v.wait_ge(dg.h, G_ARMX)
            v.reciprocal(p8[0:1, 0:3], armx[0:1, 0:3])
            v.drain()
            v.tensor_scalar_mul(p8[0:1, 0:3], p8[0:1, 0:3], QQ)
            v.drain()
            v.tensor_scalar_mul(p8[0:1, 3:5], armx[0:1, 0:2], float(1.0 / (QQ * QQ)))
            v.drain()
            v.tensor_scalar_mul(p8[0:1, 5:6], armx[0:1, 2:3], float(1.0 / QQ))
            v.drain()
            vx.inc(v.engine_nop())
        V_WSC = vx.n

        @block.tensor
        def _(t):
            t.wait_ge(vx.h, V_WSC)
            tm.inc(t.matmul(out=psX[:, 0:8], lhsT=ones64[0:1, :], rhs=p8[0:1, 0:8], start=True, stop=True))
        T_WSC = tm.n

        @block.vector
        def _(v):
            v.wait_ge(tm.h, T_WSC)
            v.tensor_copy(out=sw1s[:, :], in_=psX[:, 0:1])
            v.tensor_copy(out=sw2s[:, :], in_=psX[:, 1:2])
            v.tensor_copy(out=swats[:, :], in_=psX[:, 2:3])
            v.tensor_copy(out=sw1c[:, :], in_=psX[:, 3:4])
            v.tensor_copy(out=sw2c[:, :], in_=psX[:, 4:5])
            v.tensor_copy(out=swatd[:, :], in_=psX[:, 5:6])
            v.drain()
            vx.inc(v.engine_nop())
        V_WSC2 = vx.n

        qz_s, qz_v, qz_g, qz_ld = {}, {}, {}, {}
        blkb = blk.ap().bitcast(BF16)

        def QSTG(u):
            return blkb[:, 0:2048] if u % 2 == 0 else A(61440, 2048)
        units = ([('1', i) for i in range(8)] + [('a', i) for i in range(5)]
                 + [('2', i) for i in range(8)])
        for u, (kind, i) in enumerate(units):
            if kind == 'a':
                srcf, sc = Af(i * 2048, 2048), swats
            elif kind == '1':
                srcf, sc = Af(10240 + i * 2048, 2048), sw1s
            else:
                srcf, sc = Af((i % 2) * 2048, 2048), sw2s

            if kind == '2':
                @block.scalar
                def _(s, u=u, i=i, srcf=srcf):
                    sm_ = cq[i % 2]
                    if i < 2:
                        s.wait_ge(dgo.h, qz_g[9])
                    else:
                        s.wait_ge(dgo.h, qz_g[13 + (i - 2)])
                    s.wait_ge(sm_.h, sm_.n)
                    sm_.inc(s.dma_start(out=srcf,
                                        in_=d_w2s[i // 4][:, (i % 4) * 2048:(i % 4 + 1) * 2048]), 16)
                qz_ld[u] = (cq[i % 2].h, cq[i % 2].n)

            @block.scalar
            def _(s, u=u, kind=kind, sc=sc, srcf=srcf):
                if u == 0:
                    s.wait_ge(vx.h, V_WSC2)
                if kind == '2':
                    s.wait_ge(qz_ld[u][0], qz_ld[u][1])
                sx.inc(s.activation(out=srcf, in_=srcf, func=Act.Identity,
                                    bias=cC[:, :], scale=sc[:, :]))
            qz_s[u] = sx.n

            @block.vector
            def _(v, u=u, srcf=srcf):
                v.wait_ge(sx.h, qz_s[u])
                if u >= 2:
                    v.wait_ge(dgo.h, qz_g[u - 2])
                vx.inc(v.tensor_scalar_sub(QSTG(u), srcf, cC[:, :]))
            qz_v[u] = vx.n

            @block.gpsimd
            def _(g, u=u, kind=kind, i=i):
                g.wait_ge(vx.h, qz_v[u])
                g.wait_ge(dgo.h, dgo.n)
                if kind == 'a':
                    dst = d_waqs[i]
                elif kind == '1':
                    dst = d_w1qsA[i] if i < 4 else d_w1qsB[i - 4]
                else:
                    dst = d_w2qs[i // 4][:, (i % 4) * 2048:(i % 4 + 1) * 2048]
                dgo.inc(g.dma_start(out=dst, in_=QSTG(u)), 16)
            qz_g[u] = dgo.n

        @block.gpsimd
        def _(g):
            g.wait_ge(dgo.h, qz_g[3])
            cc.inc(g.collective_compute("AllGather", Alu.bypass, replica_groups=AR_GROUPS,
                                        ins=[d_w1qsA], outs=[d_w1q8A]))
        ST['ag_w1a'] = cc.n

        @block.gpsimd
        def _(g):
            g.wait_ge(dgo.h, qz_g[7])
            cc.inc(g.collective_compute("AllGather", Alu.bypass, replica_groups=AR_GROUPS,
                                        ins=[d_w1qsB], outs=[d_w1q8B]))
        ST['ag_w1'] = cc.n

        @block.gpsimd
        def _(g):
            g.wait_ge(dgo.h, qz_g[12])
            cc.inc(g.collective_compute("AllGather", Alu.bypass, replica_groups=AR_GROUPS,
                                        ins=[d_waqs], outs=[d_waq8.rearrange("(a b) c -> a b c", b=128)]))
        ST['ag_wa'] = cc.n

        @block.gpsimd
        def _(g):
            g.wait_ge(dgo.h, qz_g[20])
            cc.inc(g.collective_compute("AllGather", Alu.bypass, replica_groups=AR_GROUPS,
                                        ins=[d_w2qs], outs=[d_w2q8]))
        ST['ag_w2'] = cc.n

        barrier()

        def emit_blkadd(call):
            src_blk = d_x1b if call == 1 else d_x2f
            res_rows = d_xg if call == 1 else d_x2t
            dst_rows = d_x1t if call == 1 else d_out

            def DST(k):
                return Af(8192 + (k % 16) * 128, 128)

            @block.sync
            def _(s):
                for tc_ in range(NSTRIP):
                    ds.inc(s.dma_start(out=Bf(tc_ * 2048, 2048),
                                       in_=res_rows[tc_ * 128:(tc_ + 1) * 128, :]), 16)
            BA_RES = ds.n

            t0 = tm.n
            v0 = vx.n
            ld_sems = []

            @block.sync
            def _(s):
                for b in range(16):
                    sm = dsl if b % 2 == 0 else ds
                    if b >= 1:
                        s.wait_ge(sm.h, sm.n)
                    if b >= 2:
                        s.wait_ge(tm.h, t0 + 8 * (b - 2) + 8)
                    for i in range(8):
                        k = 8 * b + i
                        tc_, dc = k // NDC, k % NDC
                        sm.inc(s.dma_start(out=DST(k), in_=src_blk[dc, tc_]), 16)
                    ld_sems.append((sm.h, sm.n))

            @block.tensor
            def _(t):
                t.wait_ge(ds.h, BA_RES)
                for k in range(128):
                    smh, val = ld_sems[k // 8]
                    t.wait_ge(smh, val)
                    if k >= 2:
                        t.wait_ge(vx.h, v0 + (k - 2) + 1)
                    pb = psX if k % 2 == 0 else psAV
                    tm.inc(t.transpose(out=pb[:, 0:128], in_=DST(k), identity=identf[:, :]))

            ba_add = {}

            @block.vector
            def _(v):
                for k in range(128):
                    tc_, dc = k // NDC, k % NDC
                    v.wait_ge(tm.h, t0 + k + 1)
                    pb = psX if k % 2 == 0 else psAV
                    vx.inc(v.tensor_tensor(Bf(tc_ * 2048 + dc * 128, 128),
                                           pb[:, 0:128], Bf(tc_ * 2048 + dc * 128, 128), Alu.add))
            for k in range(128):
                ba_add[k] = v0 + k + 1

            @block.sync
            def _(s):
                for tc_ in range(NSTRIP):
                    s.wait_ge(vx.h, ba_add[tc_ * NDC + NDC - 1])
                    if tc_ >= 1:
                        s.wait_ge(dso.h, dso.n)
                    dso.inc(s.dma_start(out=dst_rows[tc_ * 128:(tc_ + 1) * 128, :],
                                        in_=Bf(tc_ * 2048, 2048)), 16)

        def emit_ffn(call):
            if call == 2:
                @block.sync
                def _(s):
                    s.wait_ge(dso.h, ST['x2t_done'])
                    for tc_ in range(NSTRIP):
                        ds.inc(s.dma_start(out=Bf(tc_ * 2048, 2048),
                                           in_=d_x2t[tc_ * 128:(tc_ + 1) * 128, :]), 16)
                ld_n = ds.n

                @block.vector
                def _(v):
                    v.wait_ge(ds.h, ld_n)
                    vx.inc(v.engine_nop())
                barrier()

            lnq_sq, lnq_vB = {}, {}
            for tc_ in range(NSTRIP):
                xs = Bf(tc_ * 2048, 2048)
                xc = Af(16384 + (tc_ % 2) * 2048, 2048)
                sq = Af(20480 + (tc_ % 2) * 2048, 2048)
                x1q = A(tc_ * 2048, 2048)

                @block.vector
                def _(v, tc_=tc_, xs=xs, xc=xc):
                    if tc_ >= 2:
                        v.wait_ge(sx.h, lnq_sq[tc_ - 2])
                    v.reduce_sum(lnm[:, tc_:tc_ + 1], xs, AX.X)
                    v.drain()
                    v.tensor_scalar_mul(lnm[:, tc_:tc_ + 1], lnm[:, tc_:tc_ + 1], 1.0 / D)
                    v.drain()
                    v.tensor_single_scalar(xc, xs, lnm[:, tc_:tc_ + 1], Alu.subtract)
                    v.drain()
                    v.reduce_max(lna[:, tc_:tc_ + 1], xc, AX.X, apply_absolute_value=True)
                    v.drain()
                    v.reciprocal(lns[:, tc_:tc_ + 1], lna[:, tc_:tc_ + 1])
                    v.drain()
                    v.tensor_scalar_mul(lns[:, tc_:tc_ + 1], lns[:, tc_:tc_ + 1], QQ)
                    v.drain()
                    vx.inc(v.engine_nop())
                vA = vx.n

                @block.scalar
                def _(s, tc_=tc_, xc=xc, sq=sq, vA=vA):
                    s.wait_ge(vx.h, vA)
                    if tc_ >= 2:
                        s.wait_ge(vx.h, lnq_vB[tc_ - 2])
                    s.drain()
                    sx.inc(s.square(out=sq, in_=xc))
                lnq_sq[tc_] = sx.n

                @block.vector
                def _(v, tc_=tc_, sq=sq):
                    v.wait_ge(sx.h, lnq_sq[tc_])
                    v.reduce_sum(lnr[:, tc_:tc_ + 1], sq, AX.X)
                    v.drain()
                    v.tensor_scalar_mul(lnr[:, tc_:tc_ + 1], lnr[:, tc_:tc_ + 1], 1.0 / D)
                    v.drain()
                    vx.inc(v.engine_nop())
                lnq_vB[tc_] = vx.n

                @block.scalar
                def _(s, tc_=tc_):
                    s.wait_ge(vx.h, lnq_vB[tc_])
                    s.drain()
                    sx.inc(s.activation(out=lnr[:, tc_:tc_ + 1], in_=lnr[:, tc_:tc_ + 1],
                                        func=Act.Sqrt, bias=epsl[:, :], scale=1.0))
                sB = sx.n

                @block.vector
                def _(v, tc_=tc_, xc=xc, x1q=x1q, sB=sB):
                    v.wait_ge(sx.h, sB)
                    v.reciprocal(lnr[:, tc_:tc_ + 1], lnr[:, tc_:tc_ + 1])
                    v.drain()
                    v.tensor_tensor(lnx[:, tc_:tc_ + 1], lna[:, tc_:tc_ + 1], lnr[:, tc_:tc_ + 1], Alu.mult)
                    v.drain()
                    v.tensor_tensor(lnx[:, tc_:tc_ + 1], lnx[:, tc_:tc_ + 1], sw1c[:, :], Alu.mult)
                    v.drain()
                    v.tensor_scalar(xc, xc, lns[:, tc_:tc_ + 1], cC[:, :], Alu.mult, Alu.add)
                    v.drain()
                    v.tensor_scalar_sub(x1q, xc, cC[:, :])
                    v.drain()
                    vx.inc(v.engine_nop())

            @block.vector
            def _(v):
                v.tensor_copy(out=p8b[:, 0:NSTRIP], in_=lnx[:, 0:NSTRIP])
                v.drain()
                vx.inc(v.engine_nop())
            vS = vx.n

            @block.tensor
            def _(t, vS=vS):
                t.wait_ge(vx.h, vS)
                tm.inc(t.transpose(out=psX[:, 0:128], in_=p8b[:, 0:128], identity=identf[:, :]))
            tS = tm.n

            @block.vector
            def _(v, tS=tS):
                v.wait_ge(tm.h, tS)
                v.tensor_copy(out=p8[:, 0:128], in_=psX[:, 0:128])
                v.drain()
                vx.inc(v.engine_nop())
            vT = vx.n

            @block.sync
            def _(s, vT=vT):
                s.wait_ge(vx.h, vT)
                dso.inc(s.dma_start(out=S1F().rearrange("a (b c) -> a b c", b=NSTRIP),
                                    in_=p8[0:NSTRIP, 0:128]), 16)
            dT = dso.n

            @block.tensor
            def _(t, dT=dT):
                t.wait_ge(dso.h, dT)
                tm.inc(t.matmul(out=psX[:, 0:512], lhsT=ones64[0:1, :], rhs=S1F()[0:1, 0:512],
                                start=True, stop=True))
                tm.inc(t.matmul(out=psS[:, 0:512], lhsT=ones64[0:1, :], rhs=S1F()[0:1, 512:1024],
                                start=True, stop=True))
            tU = tm.n

            @block.vector
            def _(v, tU=tU):
                v.wait_ge(tm.h, tU)
                v.tensor_copy(out=S1BCF1()[:, 0:512], in_=psX[:, 0:512])
                v.tensor_copy(out=S1BCF1()[:, 512:1024], in_=psS[:, 0:512])
                v.drain()
                vx.inc(v.engine_nop())

            barrier()

            t0, v0 = tm.n, vx.n

            @block.tensor
            def _(t, t0=t0, v0=v0):
                for k in range(128):
                    tc_, dc = k // NDC, k % NDC
                    if k >= 2:
                        t.wait_ge(vx.h, v0 + (k - 2) + 1)
                    pb = psTb if k % 2 == 0 else psTb2
                    tm.inc(t.transpose(out=pb[:, 0:128],
                                       in_=A(tc_ * 2048 + dc * 128, 128),
                                       identity=identb[:, :]))

            @block.vector
            def _(v, t0=t0):
                for k in range(128):
                    v.wait_ge(tm.h, t0 + k + 1)
                    tcb, dcb = k // NDC, k % NDC
                    pb = psTb if k % 2 == 0 else psTb2
                    vx.inc(v.tensor_copy(out=X1Q(dcb)[:, tcb * 128:tcb * 128 + 128],
                                         in_=pb[:, 0:128]))

            barrier()

            if DEBUG_DUMPS and call == 1:
                @block.gpsimd
                def _(g):
                    for dc in range(NDC):
                        g.wait_ge(dgo.h, dgo.n)
                        dgo.inc(g.dma_start(out=d_dbg1[dc], in_=X1Q(dc)), 16)
                barrier()

            f1_slab, f1_mm, f1_gelu, f1_chain = {}, {}, {}, {}
            f1_order = ([(4 * c_ + i_, d_w1q8A, 8 * c_ + i_) for c_ in range(8) for i_ in range(4)]
                        + [(4 * c_ + i_, d_w1q8B, 8 * c_ + 4 + i_) for c_ in range(8) for i_ in range(4)])
            for k_, (slab_ix, w1src, forig) in enumerate(f1_order):
                fc = k_
                sb_b = Bb(8192 + (fc % 3) * 2048, 2048)
                par = fc % 2

                @block.gpsimd
                def _(g, fc=fc, sb_b=sb_b, par=par, slab_ix=slab_ix, w1src=w1src):
                    sm = dg if par == 0 else dgo
                    if fc == 0:
                        g.wait_ge(cc.h, ST['ag_w1a'])
                    if fc == 32:
                        g.wait_ge(cc.h, ST['ag_w1'])
                    if fc >= 3:
                        g.wait_ge(tm.h, f1_slab[fc - 3])
                    if fc >= 2:
                        g.wait_ge(sm.h, sm.n)
                    sm.inc(g.dma_start(out=sb_b, in_=w1src[slab_ix]), 16)
                sm = dg if par == 0 else dgo
                mm_w = (sm.h, sm.n)

                ps = psS if fc % 2 == 0 else psS2
                for th in range(2):
                    @block.tensor
                    def _(t, fc=fc, th=th, sb_b=sb_b, ps=ps, mm_w=mm_w):
                        if th == 0:
                            t.wait_ge(mm_w[0], mm_w[1])
                            if fc >= 2:
                                t.wait_ge(vx.h, f1_mm[(fc - 2, 'evac')])
                        last = None
                        for c in range(NDC):
                            last = t.matmul(out=ps[:, th * 512:(th + 1) * 512],
                                            lhsT=sb_b[:, c * 128:(c + 1) * 128],
                                            rhs=X1Q(c)[:, th * 512:(th + 1) * 512],
                                            start=(c == 0), stop=(c == NDC - 1))
                        tm.inc(last)
                    f1_mm[(fc, th)] = tm.n
                    if th == 1:
                        f1_slab[fc] = tm.n

                @block.vector
                def _(v, fc=fc, ps=ps):
                    v.wait_ge(tm.h, f1_mm[(fc, 1)])
                    if fc >= 1:
                        v.wait_ge(sx.h, f1_gelu[fc - 1])
                    vx.inc(v.tensor_tensor(TT1(), ps[:, 0:1024], S1BCF1(), Alu.mult))
                f1_mm[(fc, 'evac')] = vx.n
                vB = vx.n

                @block.scalar
                def _(s, fc=fc, forig=forig, vB=vB):
                    s.wait_ge(vx.h, vB)
                    sx.inc(s.activation(out=X2(forig), in_=TT1(),
                                        func=Act.Gelu_apprx_tanh, bias=b1t[:, forig:forig + 1], scale=1.0))
                f1_gelu[fc] = sx.n

                @block.vector
                def _(v, fc=fc, forig=forig):
                    v.wait_ge(sx.h, f1_gelu[fc])
                    v.tensor_tensor(runmax[:, :], runmax[:, :], X2(forig), Alu.max)
                    v.drain()
                    v.tensor_tensor(runmin[:, :], runmin[:, :], X2(forig), Alu.min)
                    v.drain()
                    vx.inc(v.engine_nop())
                f1_chain[fc] = vx.n

            barrier()

            @block.vector
            def _(v):
                rabs = Bf(0, 1024)
                v.scalar_tensor_tensor(rabs, runmin[:, :], -1.0, runmax[:, :], Alu.mult, Alu.max)
                v.drain()
                v.memset(runmax[:, :], -3.0e38)
                v.memset(runmin[:, :], 3.0e38)
                v.drain()
                vx.inc(v.engine_nop())
            vF2 = vx.n
            tF2 = tm.n

            @block.tensor
            def _(t, vF2=vF2, tF2=tF2):
                rabs = Bf(0, 1024)
                t.wait_ge(vx.h, vF2)
                for tb in range(NSTRIP):
                    if tb >= 2:
                        t.wait_ge(vx.h, vF2 + (tb - 2) + 1)
                    pb = psX if tb % 2 == 0 else psAV
                    tm.inc(t.transpose(out=pb[:, 0:128],
                                       in_=rabs[:, tb * 128:(tb + 1) * 128], identity=identf[:, :]))

            @block.vector
            def _(v, tF2=tF2):
                for tb in range(NSTRIP):
                    v.wait_ge(tm.h, tF2 + tb + 1)
                    pb = psX if tb % 2 == 0 else psAV
                    v.reduce_max(p8[:, tb:tb + 1], pb[:, 0:128], AX.X)
                    v.drain()
                    vx.inc(v.engine_nop())
                v.tensor_single_scalar(lnx[:, 0:NSTRIP], p8[:, 0:NSTRIP], sw2c[:, :], Alu.mult)
                v.drain()
                v.tensor_scalar_mul(lnx[:, 0:NSTRIP], lnx[:, 0:NSTRIP], 0.5)
                v.drain()
                v.reciprocal(p8b[:, 0:NSTRIP], p8[:, 0:NSTRIP])
                v.drain()
                v.tensor_scalar_mul(p8b[:, 0:NSTRIP], p8b[:, 0:NSTRIP], QQ)
                v.drain()
                v.tensor_copy(out=p8b[:, 8:8 + NSTRIP], in_=lnx[:, 0:NSTRIP])
                v.drain()
                vx.inc(v.engine_nop())
            vF2b = vx.n

            @block.tensor
            def _(t, vF2b=vF2b):
                t.wait_ge(vx.h, vF2b)
                tm.inc(t.transpose(out=psX[:, 0:128], in_=p8b[:, 0:128], identity=identf[:, :]))
            tF2b = tm.n

            @block.vector
            def _(v, tF2b=tF2b):
                v.wait_ge(tm.h, tF2b)
                v.tensor_copy(out=p8[:, 0:128], in_=psX[:, 0:128])
                v.drain()
                vx.inc(v.engine_nop())
            vF2c = vx.n

            @block.sync
            def _(s, vF2c=vF2c):
                s.wait_ge(vx.h, vF2c)
                dso.inc(s.dma_start(out=S2F().rearrange("a (b c) -> a b c", b=NSTRIP),
                                    in_=p8[0:NSTRIP, 0:128]), 16)
                dso.inc(s.dma_start(out=S1F().rearrange("a (b c) -> a b c", b=NSTRIP),
                                    in_=p8[8:8 + NSTRIP, 0:128]), 16)
            dF2 = dso.n

            @block.tensor
            def _(t, dF2=dF2):
                t.wait_ge(dso.h, dF2)
                tm.inc(t.matmul(out=psX[:, 0:512], lhsT=ones64[0:1, :], rhs=S2F()[0:1, 0:512],
                                start=True, stop=True))
                tm.inc(t.matmul(out=psS[:, 0:512], lhsT=ones64[0:1, :], rhs=S2F()[0:1, 512:1024],
                                start=True, stop=True))
                tm.inc(t.matmul(out=psS2[:, 0:512], lhsT=ones64[0:1, :], rhs=S1F()[0:1, 0:512],
                                start=True, stop=True))
                tm.inc(t.matmul(out=psAV[:, 0:512], lhsT=ones64[0:1, :], rhs=S1F()[0:1, 512:1024],
                                start=True, stop=True))
            tF2c = tm.n

            @block.vector
            def _(v, tF2c=tF2c):
                v.wait_ge(tm.h, tF2c)
                v.tensor_copy(out=S2BC()[:, 0:512], in_=psX[:, 0:512])
                v.tensor_copy(out=S2BC()[:, 512:1024], in_=psS[:, 0:512])
                v.tensor_copy(out=S1BCF4()[:, 0:512], in_=psS2[:, 0:512])
                v.tensor_copy(out=S1BCF4()[:, 512:1024], in_=psAV[:, 0:512])
                v.drain()
                vx.inc(v.engine_nop())
            ST['f2_done'] = vx.n

            barrier()

            f3_v = {}
            for fc in range(NFC):
                @block.vector
                def _(v, fc=fc):
                    tt = Bf(12288 + (fc % 2) * 1024, 1024)
                    v.tensor_tensor(tt, X2(fc), S2BC()[:, :], Alu.mult)
                    v.drain()
                    vx.inc(v.tensor_scalar(X2(fc), tt, cC[:, :], cC[:, :], Alu.add, Alu.subtract))
                f3_v[fc] = vx.n
            barrier()
            if DEBUG_DUMPS and call == 1:
                @block.gpsimd
                def _(g):
                    for fc in range(NFC):
                        g.wait_ge(dgo.h, dgo.n)
                        dgo.inc(g.dma_start(out=d_dbg3[fc], in_=X2(fc)), 16)
                barrier()

            f4_slab, f4_mm, f4_res, f4_bd = {}, {}, {}, {}
            out_dst = d_x1b if call == 1 else d_x2f
            for dc in range(NDC):
                sbw = Bb((dc % 2) * 8192, 8192)
                par = dc % 2

                @block.gpsimd
                def _(g, dc=dc, sbw=sbw, par=par):
                    sm = dg if par == 0 else dgo
                    if dc == 0:
                        g.wait_ge(cc.h, ST['ag_w2'])
                    if dc < 2:
                        g.wait_ge(vx.h, ST['f2_done'])
                    if dc >= 2:
                        g.wait_ge(tm.h, f4_slab[dc - 2])
                        g.wait_ge(sm.h, sm.n)
                    sm.inc(g.dma_start(out=sbw, in_=d_w2q8[dc]), 16)
                sm = dg if par == 0 else dgo
                mm_w = (sm.h, sm.n)

                ps = psS if dc % 2 == 0 else psS2
                for th in range(2):
                    @block.tensor
                    def _(t, dc=dc, th=th, sbw=sbw, ps=ps, mm_w=mm_w):
                        if th == 0:
                            t.wait_ge(mm_w[0], mm_w[1])
                            if dc >= 2:
                                t.wait_ge(vx.h, f4_mm[(dc - 2, 'evac')])
                        last = None
                        for c in range(NFC):
                            last = t.matmul(out=ps[:, th * 512:(th + 1) * 512],
                                            lhsT=sbw[:, c * 128:(c + 1) * 128],
                                            rhs=X2(c)[:, th * 512:(th + 1) * 512],
                                            start=(c == 0), stop=(c == NFC - 1))
                        tm.inc(last)
                    f4_mm[(dc, th)] = tm.n
                    if th == 1:
                        f4_slab[dc] = tm.n

                @block.vector
                def _(v, dc=dc, ps=ps):
                    v.wait_ge(tm.h, f4_mm[(dc, 1)])
                    if dc >= 2:
                        v.wait_ge(dso.h, f4_bd[dc - 2])
                    vx.inc(v.tensor_tensor(BLK4(dc % 2), ps[:, 0:1024], S1BCF4(), Alu.mult))
                f4_mm[(dc, 'evac')] = vx.n
                vB = vx.n

                @block.scalar
                def _(s, dc=dc, vB=vB):
                    s.wait_ge(vx.h, vB)
                    sx.inc(s.activation(out=BLK4(dc % 2), in_=BLK4(dc % 2),
                                        func=Act.Identity, bias=b2h[:, dc:dc + 1], scale=1.0))
                f4_res[dc] = sx.n

                @block.sync
                def _(s, dc=dc):
                    s.wait_ge(sx.h, f4_res[dc])
                    for i in range(8):
                        if dc >= 1 or i >= 1:
                            s.wait_ge(dso.h, dso.n)
                        dso.inc(s.dma_start(out=out_dst[dc, i],
                                            in_=BLK4(dc % 2)[:, i * 128:(i + 1) * 128]), 16)
                f4_bd[dc] = dso.n

            if call == 1:
                ST['x1b_done'] = dso.n
            else:
                ST['out_done'] = dso.n
            barrier(with_queues=True)

        emit_ffn(1)

        def XN2(dc):
            return A(dc * 1024, 1024)

        def QH(h):
            return A(16384 + h * 1024, 1024)

        def APAIR(p):
            return A(16384 + 2 * p * 1024, 1024)

        def KH(kvh):
            return A(49152 + kvh * 2048, 2048)

        def VP(kc):
            return Bb(kc * 520, 520)

        def VS(vfc):
            return Bb(8320 + vfc * 2048, 2048)

        def WO(fcp):
            if fcp < 8:
                return A(fcp * 2048, 2048)
            return A(49152 + (fcp - 8) * 2048, 2048)

        def MSK(mi):
            return A(mi * 512, 512)

        emit_blkadd(1)
        barrier(with_queues=True)

        a0_ld, a0_tp, a0_ev = {}, {}, {}
        for tc_ in range(NSTRIP):
            a0_ld[tc_] = 0

        for tc_ in range(NSTRIP):
            xs = Bf(tc_ * 2048, 2048)
            sq = Af(20480 + (tc_ % 2) * 2048, 2048)
            xn2s = A(16384 + (tc_ % 2) * 2048, 2048)

            @block.vector
            def _(v, tc_=tc_, xs=xs):
                v.wait_ge(dso.h, a0_ld[tc_])
                v.reduce_sum(lnm[:, tc_:tc_ + 1], xs, AX.X)
                v.drain()
                v.tensor_scalar_mul(lnm[:, tc_:tc_ + 1], lnm[:, tc_:tc_ + 1], 1.0 / D)
                v.drain()
                vx.inc(v.engine_nop())
            vA = vx.n

            @block.scalar
            def _(s, tc_=tc_, xs=xs, sq=sq, vA=vA):
                s.wait_ge(vx.h, vA)
                if tc_ >= 2:
                    s.wait_ge(vx.h, a0_ev[(tc_ - 2, 'vB')])
                s.drain()
                sx.inc(s.activation(out=sq, in_=xs, func=Act.Square,
                                    bias=lnm[:, tc_:tc_ + 1], scale=-1.0))
            sA = sx.n

            @block.vector
            def _(v, tc_=tc_, sq=sq, sA=sA):
                v.wait_ge(sx.h, sA)
                v.reduce_sum(lnr[:, tc_:tc_ + 1], sq, AX.X)
                v.drain()
                v.tensor_scalar_mul(lnr[:, tc_:tc_ + 1], lnr[:, tc_:tc_ + 1], 1.0 / D)
                v.drain()
                vx.inc(v.engine_nop())
            a0_ev[(tc_, 'vB')] = vx.n
            vB = vx.n

            @block.scalar
            def _(s, tc_=tc_, vB=vB):
                s.wait_ge(vx.h, vB)
                s.drain()
                sx.inc(s.activation(out=lnr[:, tc_:tc_ + 1], in_=lnr[:, tc_:tc_ + 1],
                                    func=Act.Sqrt, bias=epsl[:, :], scale=1.0))
            sB = sx.n

            @block.vector
            def _(v, tc_=tc_, xs=xs, xn2s=xn2s, sB=sB):
                v.wait_ge(sx.h, sB)
                if tc_ >= 2:
                    v.wait_ge(tm.h, a0_tp[tc_ - 2])
                v.reciprocal(lnr[:, tc_:tc_ + 1], lnr[:, tc_:tc_ + 1])
                v.drain()
                v.tensor_scalar(xn2s, xs, lnm[:, tc_:tc_ + 1], lnr[:, tc_:tc_ + 1],
                                Alu.subtract, Alu.mult)
                v.drain()
                vx.inc(v.engine_nop())
            vC = vx.n

            t0 = tm.n
            v0 = vx.n

            @block.tensor
            def _(t, tc_=tc_, xn2s=xn2s, vC=vC, t0=t0, v0=v0):
                t.wait_ge(vx.h, vC)
                for dc in range(NDC):
                    if dc >= 2:
                        t.wait_ge(vx.h, v0 + (dc - 2) + 1)
                    pb = psTb if dc % 2 == 0 else psTb2
                    tm.inc(t.transpose(out=pb[:, 0:128],
                                       in_=xn2s[:, dc * 128:(dc + 1) * 128], identity=identb[:, :]))
            a0_tp[tc_] = tm.n

            @block.vector
            def _(v, tc_=tc_, t0=t0):
                for dc in range(NDC):
                    v.wait_ge(tm.h, t0 + dc + 1)
                    pb = psTb if dc % 2 == 0 else psTb2
                    vx.inc(v.tensor_copy(out=A(dc * 1024 + tc_ * 128, 128),
                                         in_=pb[:, 0:128]))
                    a0_ev[(tc_, dc)] = vx.n

        barrier(with_queues=True)

        @block.sync
        def _(s):
            ds.inc(s.dma_start(out=COSB(), in_=d_cos), 16)
            ds.inc(s.dma_start(out=SINB(), in_=d_sin), 16)
        DS_ROPE = ds.n

        @block.vector
        def _(v):
            v.wait_ge(ds.h, DS_ROPE)
            vx.inc(v.engine_nop())
        barrier()

        projs = ([('q', i) for i in range(16)] + [('k', i) for i in range(4)]
                 + [('v', i) for i in range(4)])
        WROW = {'q': 0, 'k': 2048, 'v': 2560}
        a1_lds, a1_mmv = {}, {}

        @block.gpsimd
        def _(g):
            g.wait_ge(cc.h, ST['ag_wa'])
            dg.inc(g.dma_start(out=Bb(16640, 2048), in_=d_waq8[0:128, :]), 16)
        a1_lds[0] = (dg.h, dg.n)

        a1_act, a1_rope, a1_vq, a1_pcons, a1_rcons = {}, {}, {}, {}, {}
        last_q = None
        for pi, (kind, fc) in enumerate(projs):
            bias = {'q': bqt, 'k': bkt, 'v': bvt}[kind]
            wslab = Bb(16640 + (pi % 2) * 2048, 2048)
            pstg = Bb((pi % 2) * 2048, 1024)
            prot = Bb(4096 + (pi % 2) * 1024, 1024)
            ps = psS if pi % 2 == 0 else psS2
            a1_ld = a1_lds[pi]

            for th in range(2):
                @block.tensor
                def _(t, pi=pi, th=th, wslab=wslab, ps=ps, a1_ld=a1_ld):
                    if th == 0:
                        t.wait_ge(a1_ld[0], a1_ld[1])
                        if pi >= 2:
                            t.wait_ge(sx.h, a1_act[pi - 2])
                    last = None
                    for dc in range(NDC):
                        last = t.matmul(out=ps[:, th * 512:(th + 1) * 512],
                                        lhsT=wslab[:, dc * 128:(dc + 1) * 128],
                                        rhs=XN2(dc)[:, th * 512:(th + 1) * 512],
                                        start=(dc == 0), stop=(dc == NDC - 1))
                    tm.inc(last)
            a1_mm = tm.n

            @block.scalar
            def _(s, pi=pi, fc=fc, pstg=pstg, ps=ps, bias=bias, a1_mm=a1_mm):
                s.wait_ge(tm.h, a1_mm)
                if pi >= 2:
                    hd, val = a1_pcons[pi - 2]
                    s.wait_ge(hd, val)
                sx.inc(s.activation(out=pstg[:, 0:1024], in_=ps[:, 0:1024],
                                    func=Act.Identity, bias=bias[:, fc:fc + 1], scale=swatd[:, :]))
            a1_act[pi] = sx.n

            if kind in ('q', 'k'):
                @block.tensor
                def _(t, pi=pi, pstg=pstg):
                    t.wait_ge(sx.h, a1_act[pi])
                    if a1_rope:
                        t.wait_ge(vx.h, max(a1_rope.values()))
                    tm.inc(t.matmul(out=psX[:, 0:512], lhsT=rotT[:, :], rhs=pstg[:, 0:512],
                                    start=True, stop=True))
                    tm.inc(t.matmul(out=psAV[:, 0:512], lhsT=rotT[:, :], rhs=pstg[:, 512:1024],
                                    start=True, stop=True))
                tR = tm.n

                @block.vector
                def _(v, pi=pi, pstg=pstg, prot=prot, tR=tR):
                    for th in range(2):
                        v.wait_ge(tm.h, tR - 1 + th)
                        if th == 0 and pi >= 2 and (pi - 2) in a1_rcons:
                            hd, val = a1_rcons[pi - 2]
                            v.wait_ge(hd, val)
                        psrc = psX if th == 0 else psAV
                        m1 = Bf(10368 + th * 512, 512)
                        m2 = Bf(11392 + th * 512, 512)
                        v.tensor_tensor(m1, pstg[:, th * 512:(th + 1) * 512],
                                        COSB()[:, th * 512:(th + 1) * 512], Alu.mult)
                        v.tensor_tensor(m2, psrc[:, 0:512], SINB()[:, th * 512:(th + 1) * 512], Alu.mult)
                        v.drain()
                        v.tensor_tensor(prot[:, th * 512:(th + 1) * 512], m1, m2, Alu.add)
                        v.drain()
                    vx.inc(v.engine_nop())
                a1_rope[pi] = vx.n
                ropeV = vx.n

                if kind == 'q':
                    @block.sync
                    def _(s, fc=fc, prot=prot, ropeV=ropeV):
                        s.wait_ge(vx.h, ropeV)
                        s.wait_ge(dso.h, dso.n)
                        dso.inc(s.dma_start(out=QH(2 * fc)[0:64, :], in_=prot[0:64, :]), 16)
                        dso.inc(s.dma_start(out=QH(2 * fc + 1)[0:64, :], in_=prot[64:128, :]), 16)
                    a1_rcons[pi] = (dso.h, dso.n)
                    a1_pcons[pi] = (vx.h, ropeV)
                else:
                    @block.gpsimd
                    def _(g, fc=fc, prot=prot, ropeV=ropeV):
                        g.wait_ge(vx.h, ropeV)
                        g.wait_ge(dgo.h, dgo.n)
                        dgo.inc(g.dma_start(out=d_agi[fc * 128:(fc + 1) * 128, :], in_=prot), 16)
                    a1_rcons[pi] = (dgo.h, dgo.n)
                    a1_pcons[pi] = (vx.h, ropeV)
            else:
                @block.gpsimd
                def _(g, pi=pi, fc=fc, pstg=pstg):
                    g.wait_ge(sx.h, a1_act[pi])
                    g.wait_ge(dgo.h, dgo.n)
                    dgo.inc(g.dma_start(out=d_agi[512 + fc * 128:512 + (fc + 1) * 128, :], in_=pstg), 16)
                a1_pcons[pi] = (dgo.h, dgo.n)

            a1_mmv[pi] = a1_mm
            if pi + 1 < len(projs):
                nkind, nfc = projs[pi + 1]
                row0 = WROW[nkind] + nfc * 128

                @block.gpsimd
                def _(g, pi=pi, row0=row0):
                    sm_ = dg if (pi + 1) % 2 == 0 else dga
                    if pi >= 1:
                        g.wait_ge(tm.h, a1_mmv[pi - 1])
                        g.wait_ge(sm_.h, sm_.n)
                    sm_.inc(g.dma_start(out=Bb(16640 + ((pi + 1) % 2) * 2048, 2048),
                                        in_=d_waq8[row0:row0 + 128, :]), 16)
                sm_ = dg if (pi + 1) % 2 == 0 else dga
                a1_lds[pi + 1] = (sm_.h, sm_.n)

        barrier(with_queues=True)

        @block.gpsimd
        def _(g):
            g.wait_ge(dgo.h, dgo.n)
            cc.inc(g.collective_compute("AllGather", Alu.bypass,
                                        replica_groups=AG_GROUPS,
                                        ins=[d_agi], outs=[d_ago]))
            g.wait_ge(cc.h, cc.n)
        CC_AG = cc.n

        @block.sync
        def _(s):
            for mi in range(16):
                ds.inc(s.dma_start(out=MSK(mi), in_=d_msk[mi]), 16)

        @block.vector
        def _(v):
            v.memset(Bb(0, 16 * 520), 1.0)
            v.drain()
            for kvh in range(KVH):
                v.memset(KH(kvh)[64:65, :], 1.0)
            v.drain()
            vx.inc(v.engine_nop())
        V_ONES = vx.n

        KMAP = [(0, 0, 0), (0, 512, 1536), (1, 0, 512), (1, 512, 1024)]

        @block.gpsimd
        def _(g):
            g.wait_ge(cc.h, CC_AG)
            g.wait_ge(vx.h, V_ONES)
            for kvh in range(KVH):
                for (sl, c0, k0) in KMAP:
                    dg.inc(g.dma_start(out=KH(kvh)[0:64, k0:k0 + 512],
                                       in_=d_ago[sl * 1024 + kvh * 64:sl * 1024 + (kvh + 1) * 64,
                                                 c0:c0 + 512]), 16)
        DG_K = dg.n

        @block.sync
        def _(s):
            s.wait_ge(cc.h, CC_AG)
            s.wait_ge(vx.h, V_ONES)
            for vfc in range(4):
                for (sl, c0, k0) in KMAP:
                    dso.inc(s.dma_start(out=VS(vfc)[:, k0:k0 + 512],
                                        in_=d_ago[sl * 1024 + 512 + vfc * 128:
                                                  sl * 1024 + 512 + (vfc + 1) * 128, c0:c0 + 512]), 16)
        D_VST = dso.n

        tV0 = tm.n
        vV0 = vx.n

        @block.tensor
        def _(t):
            t.wait_ge(dso.h, D_VST)
            t.wait_ge(vx.h, V_ONES)
            for vfc in range(4):
                for kc in range(16):
                    k = vfc * 16 + kc
                    if k >= 2:
                        t.wait_ge(vx.h, vV0 + (k - 2) + 1)
                    pb = psTb if k % 2 == 0 else psTb2
                    tm.inc(t.transpose(out=pb[:, 0:128],
                                       in_=VS(vfc)[:, kc * 128:(kc + 1) * 128],
                                       identity=identb[:, :]))

        @block.vector
        def _(v, tV0=tV0):
            for vfc in range(4):
                for kc in range(16):
                    k = vfc * 16 + kc
                    v.wait_ge(tm.h, tV0 + k + 1)
                    pb = psTb if k % 2 == 0 else psTb2
                    v.tensor_copy(out=VP(kc)[:, (2 * vfc) * 65:(2 * vfc) * 65 + 64],
                                  in_=pb[:, 0:64])
                    v.tensor_copy(out=VP(kc)[:, (2 * vfc + 1) * 65:(2 * vfc + 1) * 65 + 64],
                                  in_=pb[:, 64:128])
                    v.drain()
                    vx.inc(v.engine_nop())

        barrier(with_queues=True)

        @block.vector
        def _(v):
            v.memset(runmax[:, :], -3.0e38)
            v.memset(runmin[:, :], 3.0e38)
            v.drain()
            vx.inc(v.engine_nop())

        barrier(with_queues=True)

        NB = len(BATCHES)
        flat = [(h, bi, batch) for h in range(NH) for bi, batch in enumerate(BATCHES)]
        b_mm, b_exp, b_msk, b_av, evac_v = {}, {}, {}, {}, {}

        def emit_av(g):
            h, bi, batch = flat[g]
            kvh = h // 4
            ach = ATTCH(g % 4)

            @block.tensor
            def _(t, h=h, g=g, batch=batch, ach=ach, kvh=kvh):
                kd, val = b_msk[g]
                t.wait_ge(vx.h if kd == 'v' else sx.h, val)
                for i, sl in enumerate(batch):
                    qg, kc = SLOTS[sl]
                    if sl == 0 and h >= 1:
                        t.wait_ge(vx.h, evac_v[(h - 1, 1)])
                    if sl == 8:
                        t.wait_ge(vx.h, evac_v[(h, 0)])
                    tm.inc(t.matmul(out=psAV[0:65, 0:512],
                                    lhsT=VP(kc)[:, kvh * 65:(kvh + 1) * 65],
                                    rhs=ach[:, i * 512:(i + 1) * 512],
                                    start=(sl == 0 or sl == 8),
                                    stop=(sl == 7 or sl == 23),
                                    skip_group_check=True))
            b_av[g] = tm.n

            for qg_end, qg in ((7, 0), (23, 1)):
                if batch[-1] == qg_end:
                    @block.vector
                    def _(v, g=g, qg=qg):
                        v.wait_ge(tm.h, b_av[g])
                        with nc.allow_low_precision(reason="softmax denom reciprocal broadcast in bf16"):
                            vx.inc(v.reciprocal(REC64B()[64:65, :], psAV[64:65, 0:512]))
                    vR = vx.n

                    @block.tensor
                    def _(t, vR=vR):
                        t.wait_ge(vx.h, vR)
                        tm.inc(t.matmul(out=psX[:, 0:512], lhsT=onesb[64:65, 0:128],
                                        rhs=REC64B()[64:65, :], start=True, stop=True))
                    tB = tm.n

                    @block.vector
                    def _(v, h=h, qg=qg, tB=tB):
                        v.wait_ge(tm.h, tB)
                        if h % 2 == 1 and qg == 1:
                            v.wait_ge(dso.h, ST[('aodd_d', h)])
                        if h % 2 == 1 and qg == 0 and h >= 3:
                            v.wait_ge(dso.h, ST[('aodd_d2', h - 2)])
                        v.tensor_copy(out=RECB()[0:64, :], in_=psX[0:64, 0:512])
                        v.drain()
                        if h % 2 == 0:
                            v.tensor_tensor(APAIR(h // 2)[0:64, qg * 512:(qg + 1) * 512],
                                            psAV[0:64, 0:512], RECB()[0:64, :], Alu.mult)
                        else:
                            v.tensor_tensor(AODD()[0:64, :],
                                            psAV[0:64, 0:512], RECB()[0:64, :], Alu.mult)
                        v.drain()
                        vx.inc(v.engine_nop())
                    evac_v[(h, qg)] = vx.n

                    if h % 2 == 1:
                        @block.sync
                        def _(s, h=h, qg=qg):
                            s.wait_ge(vx.h, evac_v[(h, qg)])
                            if qg == 1:
                                s.wait_ge(dso.h, dso.n)
                            dso.inc(s.dma_start(out=APAIR(h // 2)[64:128, qg * 512:(qg + 1) * 512],
                                                in_=AODD()[0:64, :]), 16)
                        if qg == 0:
                            ST[('aodd_d', h)] = dso.n
                        else:
                            ST[('aodd_d2', h)] = dso.n

        for g, (h, bi, batch) in enumerate(flat):
            kvh = h // 4
            psb = psS if g % 2 == 0 else psS2
            ach = ATTCH(g % 4)

            @block.tensor
            def _(t, h=h, g=g, batch=batch, psb=psb, kvh=kvh):
                if g >= 2:
                    t.wait_ge(sx.h, b_exp[g - 2])
                for i, sl in enumerate(batch):
                    qg, kc = SLOTS[sl]
                    tm.inc(t.matmul(out=psb[:, i * 512:(i + 1) * 512],
                                    lhsT=KH(kvh)[0:64, kc * 128:(kc + 1) * 128],
                                    rhs=QH(h)[0:64, qg * 512:(qg + 1) * 512],
                                    start=True, stop=True))
            b_mm[g] = tm.n

            @block.scalar
            def _(s, g=g, batch=batch, psb=psb, ach=ach):
                s.wait_ge(tm.h, b_mm[g])
                if g >= 4:
                    s.wait_ge(tm.h, b_av[g - 4])
                n = len(batch) * 512
                sx.inc(s.activation(out=ach[:, 0:n], in_=psb[:, 0:n], func=Act.Exp,
                                    bias=0.0, scale=0.125))
            b_exp[g] = sx.n

            if any(sl in MASKIDX for sl in batch):
                @block.vector
                def _(v, g=g, batch=batch, ach=ach):
                    v.wait_ge(sx.h, b_exp[g])
                    for i, sl in enumerate(batch):
                        if sl in MASKIDX:
                            v.tensor_tensor(ach[:, i * 512:(i + 1) * 512],
                                            ach[:, i * 512:(i + 1) * 512],
                                            MSK(MASKIDX[sl]), Alu.mult)
                    v.drain()
                    vx.inc(v.engine_nop())
                b_msk[g] = ('v', vx.n)
            else:
                b_msk[g] = ('s', b_exp[g])

            if g >= 1:
                emit_av(g - 1)
        emit_av(len(flat) - 1)

        barrier(with_queues=True)

        @block.sync
        def _(s):
            ds.inc(s.dma_start(out=BOROW(), in_=d_bo), 16)
        DS_BO = ds.n

        @block.vector
        def _(v):
            v.wait_ge(ds.h, DS_BO)
            vx.inc(v.engine_nop())
        barrier()

        for i in range(4):
            @block.tensor
            def _(t, i=i):
                if i >= 1:
                    t.wait_ge(vx.h, ST[('bobc', i - 1)])
                tm.inc(t.matmul(out=psX[:, 0:512], lhsT=ones64[0:1, :],
                                rhs=BOROW()[0:1, i * 512:(i + 1) * 512], start=True, stop=True))
            tO = tm.n

            @block.vector
            def _(v, i=i, tO=tO):
                v.wait_ge(tm.h, tO)
                v.tensor_copy(out=BOBC()[:, i * 512:(i + 1) * 512], in_=psX[:, 0:512])
                v.drain()
                vx.inc(v.engine_nop())
            ST[('bobc', i)] = vx.n
        V_BOBC = vx.n

        wo_ld = {}
        for fcp in range(NDC):
            @block.gpsimd
            def _(g, fcp=fcp):
                if fcp == 0:
                    g.wait_ge(vx.h, V_BOBC)
                sm_ = dg if fcp % 2 == 0 else dga
                if fcp >= 2:
                    g.wait_ge(sm_.h, sm_.n)
                r0 = 3072 + fcp * 128
                sm_.inc(g.dma_start(out=WO(fcp), in_=d_waq8[r0:r0 + 128, :]), 16)
            sm_ = dg if fcp % 2 == 0 else dga
            wo_ld[fcp] = (sm_.h, sm_.n)

        wo_sc = {}
        for fcp in range(NDC):
            @block.vector
            def _(v, fcp=fcp):
                v.wait_ge(wo_ld[fcp][0], wo_ld[fcp][1])
                with nc.allow_low_precision(reason="int8 Wo dequant scale in bf16"):
                    vx.inc(v.tensor_single_scalar(WO(fcp), WO(fcp), swatd[:, 0:1], Alu.mult))
            wo_sc[fcp] = vx.n
        DG_WO = None

        barrier(with_queues=True)

        a3_mm, a3_v, a3_bd = {}, {}, {}
        for tc_ in range(NSTRIP):
            x1sl = Bf(8192 + (tc_ % 2) * 2048, 2048)
            x2sl = Bf(12288 + (tc_ % 2) * 2048, 2048)

            @block.sync
            def _(s, tc_=tc_, x1sl=x1sl):
                if tc_ >= 2:
                    s.wait_ge(vx.h, a3_v[(tc_ - 2, 3)])
                if tc_ >= 1:
                    s.wait_ge(dso.h, dso.n)
                dso.inc(s.dma_start(out=x1sl, in_=d_x1t[tc_ * 128:(tc_ + 1) * 128, :]), 16)
            a3_x1 = dso.n

            for dh in range(4):
                ps = psS if dh % 2 == 0 else psS2
                @block.tensor
                def _(t, tc_=tc_, dh=dh, ps=ps):
                    if dh >= 2:
                        t.wait_ge(vx.h, a3_v[(tc_, dh - 2)])
                    elif tc_ >= 1:
                        t.wait_ge(vx.h, a3_v[(tc_ - 1, dh + 2)])
                    last = None
                    for fcp in range(NDC):
                        last = t.matmul(out=ps[:, 0:512], lhsT=APAIR(fcp)[:, tc_ * 128:(tc_ + 1) * 128],
                                        rhs=WO(fcp)[:, dh * 512:(dh + 1) * 512],
                                        start=(fcp == 0), stop=(fcp == NDC - 1))
                    tm.inc(last)
                a3_mm[(tc_, dh)] = tm.n

                @block.vector
                def _(v, tc_=tc_, dh=dh, ps=ps, x1sl=x1sl, x2sl=x2sl):
                    v.wait_ge(tm.h, a3_mm[(tc_, dh)])
                    v.wait_ge(dso.h, a3_x1)
                    if tc_ >= 2:
                        v.wait_ge(dso.h, a3_bd[(tc_ - 2, 'x2t')])
                    v.scalar_tensor_tensor(x2sl[:, dh * 512:(dh + 1) * 512], ps[:, 0:512], 1.0,
                                           BOBC()[:, dh * 512:(dh + 1) * 512], Alu.mult, Alu.add)
                    v.drain()
                    v.tensor_tensor(x2sl[:, dh * 512:(dh + 1) * 512],
                                    x2sl[:, dh * 512:(dh + 1) * 512],
                                    x1sl[:, dh * 512:(dh + 1) * 512], Alu.add)
                    v.drain()
                    vx.inc(v.engine_nop())
                a3_v[(tc_, dh)] = vx.n

            @block.sync
            def _(s, tc_=tc_, x2sl=x2sl):
                for dh in range(4):
                    s.wait_ge(vx.h, a3_v[(tc_, dh)])
                dso.inc(s.dma_start(out=d_x2t[tc_ * 128:(tc_ + 1) * 128, :], in_=x2sl), 16)
            a3_bd[(tc_, 'x2t')] = dso.n

        ST['x2t_done'] = max(a3_bd[(tc_, 'x2t')] for tc_ in range(NSTRIP))
        barrier(with_queues=True)

        emit_ffn(2)

        emit_blkadd(2)
        barrier(with_queues=True)

    return nc


def _host_prep(inputs):
    _CACHE = {}
    W1 = np.ascontiguousarray(inputs["L0_ffn_W1"], dtype=np.float32)
    W2 = np.ascontiguousarray(inputs["L0_ffn_W2"], dtype=np.float32)
    Wq = np.ascontiguousarray(inputs["L0_attn_Wq"], dtype=np.float32)
    Wk = np.ascontiguousarray(inputs["L0_attn_Wk"], dtype=np.float32)
    Wv = np.ascontiguousarray(inputs["L0_attn_Wv"], dtype=np.float32)
    Wo = np.ascontiguousarray(inputs["L0_attn_Wo"], dtype=np.float32)
    C = _CACHE
    C['w1b'] = np.ascontiguousarray(W1.reshape(64, 128, 16, 128).transpose(0, 3, 2, 1)).reshape(64, 128, 2048)
    C['w2b'] = np.ascontiguousarray(W2.reshape(16, 128, 64, 128).transpose(0, 3, 2, 1)).reshape(16, 128, 8192)
    C['wqb'] = np.ascontiguousarray(Wq.reshape(16, 128, 16, 128).transpose(2, 1, 0, 3)).reshape(16, 128, 2048)
    C['wkb'] = np.ascontiguousarray(Wk.reshape(16, 128, 4, 128).transpose(2, 1, 0, 3)).reshape(4, 128, 2048)
    C['wvb'] = np.ascontiguousarray(Wv.reshape(16, 128, 4, 128).transpose(2, 1, 0, 3)).reshape(4, 128, 2048)
    C['wob'] = np.ascontiguousarray(Wo.reshape(16, 128, 2048))
    C['waflat'] = np.ascontiguousarray(np.concatenate(
        [C['wqb'].reshape(2048, 2048), C['wkb'].reshape(512, 2048),
         C['wvb'].reshape(512, 2048), C['wob'].reshape(2048, 2048)], axis=0))
    C['b1t'] = np.ascontiguousarray(inputs["L0_ffn_B1"].reshape(64, 128).T, dtype=np.float32)
    C['b2t'] = np.ascontiguousarray(inputs["L0_ffn_B2"].reshape(16, 128).T, dtype=np.float32)
    C['bqt'] = np.ascontiguousarray(inputs["L0_attn_bq"].reshape(16, 128).T, dtype=np.float32)
    C['bkt'] = np.ascontiguousarray(inputs["L0_attn_bk"].reshape(4, 128).T, dtype=np.float32)
    C['bvt'] = np.ascontiguousarray(inputs["L0_attn_bv"].reshape(4, 128).T, dtype=np.float32)
    C['bor'] = np.ascontiguousarray(inputs["L0_attn_bo"].reshape(1, 2048), dtype=np.float32)
    C['idf'] = np.eye(128, dtype=np.float32)
    C['idb'] = np.eye(128, dtype=BFD)
    P = np.zeros((128, 128), dtype=np.float32)
    for i in range(64):
        P[2 * i, 2 * i + 1] = -1.0
        P[2 * i + 1, 2 * i] = 1.0
    C['rotT'] = np.ascontiguousarray(P.T, dtype=BFD)
    inv = (1.0 / (10000.0 ** (np.arange(0, HD, 2, dtype=np.float32) / HD))).astype(np.float32)
    cosd = {}
    for j in (0, 1):
        pos = _pos_list(j).astype(np.float32)
        ang = np.outer(inv.repeat(2), pos)
        cosv = np.cos(ang).astype(np.float32)
        sinv = np.sin(ang).astype(np.float32)
        cosd[j] = (np.tile(cosv, (2, 1)).astype(BFD), np.tile(sinv, (2, 1)).astype(BFD))
    C['rope'] = cosd
    msks = {}
    for j in (0, 1):
        pos = _pos_list(j)
        tiles = np.zeros((16, 128, 512), dtype=BFD)
        for mi, sl in enumerate(MASKED):
            qg, kc = SLOTS[sl]
            q = pos[qg * 512:(qg + 1) * 512][None, :]
            k = (np.arange(128) + kc * 128)[:, None]
            tiles[mi] = (q >= k).astype(np.float32).astype(BFD)
        msks[j] = tiles
    C['msk'] = msks
    return C


def kernel(**inputs):
    C = _host_prep(inputs)
    x = np.ascontiguousarray(inputs["L0_attn_x"], dtype=np.float32)
    nc = build_program()
    in_maps = []
    for c in range(8):
        b, j = c // 2, c % 2
        pos = _pos_list(j)
        m = dict(
            xg=np.ascontiguousarray(x[b][pos]),
            w1s=np.ascontiguousarray(C['w1b'][8 * c:8 * c + 8]),
            w2s=np.ascontiguousarray(C['w2b'][2 * c:2 * c + 2]),
            was=np.ascontiguousarray(C['waflat'][640 * c:640 * (c + 1)].reshape(5, 128, D)),
            b1t=C['b1t'], b2t=C['b2t'], bqt=C['bqt'], bkt=C['bkt'], bvt=C['bvt'], bor=C['bor'],
            cosb=C['rope'][j][0], sinb=C['rope'][j][1],
            idf=C['idf'], idb=C['idb'], rotT=C['rotT'], msk=C['msk'][j],
        )
        in_maps.append(m)
    res = run_bass_kernel_spmd(nc, in_maps, core_ids=list(range(8)))
    out = np.empty((B, T, D), dtype=np.float32)
    for c in range(8):
        b, j = c // 2, c % 2
        pos = _pos_list(j)
        out[b][pos] = np.asarray(res.results[c]["outb"])
    return out
